# version 20 (final)
# baseline (speedup 1.0000x reference)
import numpy as np
import concourse.bass as bass
import concourse.mybir as mybir
from concourse.bass_utils import run_bass_kernel_spmd
from contextlib import ExitStack

F32 = mybir.dt.float32
BF16 = mybir.dt.bfloat16
I32 = mybir.dt.int32
AF = mybir.ActivationFunctionType
ALU = mybir.AluOpType
AX = mybir.AxisListType

NCORE = 8
D = 1024
T = 16384
DI = 2816
TQ = 1024
NT = 16
KS = 128
NKS = 43
NSEG = 43
SEGW = 389
IC = DI // 128
TPC = T // NCORE
MAGIC = 3.0 * 2.0 ** 22
QMAX = 127.0

SUBS = [[s for s in range(NKS) if 3 * KS * s <= TQ * t + TQ - 1] for t in range(NT)]

def qx_of(t, s):
    return 3 * KS * s - TQ * t

def ql_of(t, s):
    q = qx_of(t, s)
    return 0 if q <= 0 else min(TQ, (q // 512) * 512)

def qhb_of(t, s):
    return min(TQ, 3 * (KS * s + 127) + 1 - TQ * t)

def pv_c0(t, s):
    return max(0, qx_of(t, s)) // 512

def last_sub(t, c):
    return max(s for s in SUBS[t] if pv_c0(t, s) <= c)

def off(c, h):
    return (2 * c + h) * 512

def mask_pieces(t, s):
    b0 = max(0, qx_of(t, s))
    b1 = min(TQ, qhb_of(t, s))
    if b1 <= b0:
        return []
    out = []
    for h in range(2):
        a = b0
        while a < b1:
            b = min(b1, (a // 512 + 1) * 512)
            out.append((h, a, b))
            a = b
    return out

TORDER = [12, 13, 14, 15, 8, 9, 10, 11, 4, 5, 6, 7, 0, 1, 2, 3]
PREV_T = {TORDER[i]: TORDER[i - 1] for i in range(1, 16)}
TPOS = {t: i for i, t in enumerate(TORDER)}
GID = {}
_g = 0
for _t in TORDER:
    for _i in range(len(SUBS[_t])):
        _g += 1
        GID[(_t, _i)] = _g
NG = _g

QIT = [(w, ic) for ic in range(IC) for w in ("g", "u")] + [("d", ic) for ic in range(IC)]
QSC = [("g", ic) for ic in range(IC)] + [("u", ic) for ic in range(IC)] + [("d", hc) for hc in range(8)]
QSEG = [min(NSEG - 1, qi * NSEG // 52) for qi in range(52)]
QB = [[] for _ in range(NT)]
for _qi in range(len(QIT)):
    QB[min(NT - 1, _qi * NT // len(QIT))].append(_qi)

def seg_cols(s):
    tk0 = 384 * s - 3
    a = max(0, tk0)
    b = min(T, tk0 + SEGW)
    return a, b - a, a - tk0

def seg_n_q(s):
    return min(384, T - 384 * s)


class _FakeInst:
    def then_inc(self, sem, k=1):
        return self

_FAKE = _FakeInst()

class _MockEng:
    def __getattr__(self, name):
        if name == "value_load":
            return lambda *a, **k: 0
        return lambda *a, **k: _FAKE

_MOCK = _MockEng()


class Ctx:
    def __init__(self):
        self.mock = True
        self.cnt = {}
        self.marks = {}
        self.sems = {}

    def inc(self, inst, sem, k=1):
        self.cnt[sem] = self.cnt.get(sem, 0) + k
        if not self.mock:
            inst.then_inc(self.sems[sem], k)
        return inst

    def mark(self, key, sem):
        v = self.cnt.get(sem, 0)
        if self.mock:
            assert key not in self.marks, key
            self.marks[key] = (sem, v)
        else:
            assert self.marks[key] == (sem, v), (key, self.marks[key], (sem, v))

    def wait(self, eng, key):
        if not self.mock:
            sem, v = self.marks[key]
            if v > 0:
                eng.wait_ge(self.sems[sem], v)

    def selfwait(self, eng, sem):
        if not self.mock and self.cnt.get(sem, 0) > 0:
            eng.wait_ge(self.sems[sem], self.cnt[sem])

    def gdma(self, g, **kw):
        self.selfwait(g, "lg")
        inst = g.dma_start(**kw)
        return self.inc(inst, "lg", 16)

    def sdma(self, sy, key=None, **kw):
        i = self.cnt.get("_sdseq", 0)
        self.cnt["_sdseq"] = i + 1
        sem = f"ld{i % 8}"
        self.selfwait(sy, sem)
        inst = sy.dma_start(**kw)
        self.inc(inst, sem, 16)
        if key is not None:
            self.mark(key, sem)
        return inst

    def sdrain(self, sy, key=None):
        for j in range(8):
            self.selfwait(sy, f"ld{j}")
        if key is not None:
            self.mark(key, "ld0")

    def barrier_inc(self, eng):
        self.cnt["ph"] = self.cnt.get("ph", 0) + 1
        if not self.mock:
            eng.sem_inc(self.sems["ph"], 1)

    def barrier_wait(self, eng, v):
        if not self.mock:
            eng.wait_ge(self.sems["ph"], v)


def build_nc(stage=None):
    nc = bass.Bass("TRN2", target_bir_lowering=False)

    xT = nc.dram_tensor("xT", [D, T], F32, kind="ExternalInput")
    xres = nc.dram_tensor("xres", [D, TPC], F32, kind="ExternalInput")
    cid = nc.dram_tensor("cid", [1, 1], I32, kind="ExternalInput")
    coff = nc.dram_tensor("coff", [1, 4], I32, kind="ExternalInput")
    wkTc = nc.dram_tensor("wkTc", [D, 128], F32, kind="ExternalInput")
    wvTc = nc.dram_tensor("wvTc", [D, 128], F32, kind="ExternalInput")
    wqTc = nc.dram_tensor("wqTc", [D, 128], F32, kind="ExternalInput")
    wconvN = nc.dram_tensor("wconvN", [D, 3072], F32, kind="ExternalInput")
    woT = nc.dram_tensor("woT", [D, D], F32, kind="ExternalInput")
    bo = nc.dram_tensor("bo", [D], F32, kind="ExternalInput")
    wgN = nc.dram_tensor("wgN", [DI, D], F32, kind="ExternalInput")
    wgT = nc.dram_tensor("wgT", [D, DI], F32, kind="ExternalInput")
    wuN = nc.dram_tensor("wuN", [DI, D], F32, kind="ExternalInput")
    wuT = nc.dram_tensor("wuT", [D, DI], F32, kind="ExternalInput")
    wdN = nc.dram_tensor("wdN", [D, DI], F32, kind="ExternalInput")
    wdT = nc.dram_tensor("wdT", [DI, D], F32, kind="ExternalInput")
    clipx = nc.dram_tensor("clipx", [1], F32, kind="ExternalInput")
    clipd = nc.dram_tensor("clipd", [1], F32, kind="ExternalInput")

    dbg = {}
    if stage is None:
        outT = nc.dram_tensor("outT", [D, TPC], F32, kind="ExternalOutput")
    elif stage == "A":
        dbg["kh"] = nc.dram_tensor("dbg_kh", [128, NKS * KS], BF16, kind="ExternalOutput")
        dbg["va"] = nc.dram_tensor("dbg_va", [128, NKS * 256], BF16, kind="ExternalOutput")
        dbg["qh"] = nc.dram_tensor("dbg_qh", [128, T], BF16, kind="ExternalOutput")
        dbg["wq"] = nc.dram_tensor("dbg_wq", [IC * 8 * 128 * 128], BF16, kind="ExternalOutput")
        dbg["wd"] = nc.dram_tensor("dbg_wd", [IC * 8 * 128 * 128], BF16, kind="ExternalOutput")
    elif stage == "B":
        dbg["oin"] = nc.dram_tensor("dbg_oin", [128, T], BF16, kind="ExternalOutput")

    o_in = nc.dram_tensor("o_in", [8, 128, T // 8], BF16)
    o_ag = nc.dram_tensor("o_ag", [8, NCORE * 128, T // 8], BF16)
    ar1_in = nc.dram_tensor("ar1_in", [128], F32)
    ar1_out = nc.dram_tensor("ar1_out", [128], F32)
    ar2_in = nc.dram_tensor("ar2_in", [128], F32)
    ar2_out = nc.dram_tensor("ar2_out", [128], F32)
    xln_d = nc.dram_tensor("xln_d", [D, TPC], F32)
    h_d = nc.dram_tensor("h_d", [DI, TPC], F32)
    wq_d = {"g": nc.dram_tensor("wgq_d", [IC * 8 * 128 * 128], BF16),
            "u": nc.dram_tensor("wuq_d", [IC * 8 * 128 * 128], BF16),
            "d": nc.dram_tensor("wdq_d", [IC * 8 * 128 * 128], BF16)}
    rs_d = nc.dram_tensor("rs_d", [3 * DI], F32)
    v0_d = nc.dram_tensor("v0_d", [128], F32)
    sc_d = nc.dram_tensor("sc_d", [16], F32)

    ctx = Ctx()
    perm = ExitStack()
    def psb(name, shape, dt):
        return perm.enter_context(nc.sbuf_tensor(name, shape, dt))
    sg_all = psb("sg_all", [128, IC], F32)
    su_all = psb("su_all", [128, IC], F32)
    sd_all = psb("sd_all", [128, 8], F32)
    ones128 = psb("ones128", [128, 128], F32)
    cid_sb = psb("cid_sb", [1, 1], I32)
    coff_sb = psb("coff_sb", [1, 4], I32)
    scal = psb("scal", [128, 16], F32)
    rmax = psb("rmax", [128, 2], F32)
    magic_sb = psb("magic_sb", [128, 1], F32)
    rrow = psb("rrow", [1, 130], F32)
    sgi = psb("sgi", [128, IC], F32)
    sui = psb("sui", [128, IC], F32)
    sdi = psb("sdi", [128, 8], F32)
    for s in ("lg", "tm", "vm", "sm", "gm", "ph", "sy",
              "ld0", "ld1", "ld2", "ld3", "ld4", "ld5", "ld6", "ld7"):
        ctx.sems[s] = perm.enter_context(nc.semaphore(s))

    stAB = ExitStack()
    def ab(name, shape, dt):
        return stAB.enter_context(nc.sbuf_tensor(name, shape, dt))
    q_h = ab("q_h", [128, T], BF16)
    k_h = ab("k_h", [128, NKS * KS + 128], BF16)
    v_aug = ab("v_aug", [128, NKS * 256], BF16)
    m384 = ab("m384", [128, 384], BF16)
    wtile = ab("wtile", [128, 4 * 1024], F32)
    rs_bc = ab("rs_bc", [128, 4 * 1024], F32)
    wqb = ab("wqb", [128, 6 * 1024], BF16)
    rsg_all = ab("rsg_all", [128, IC], F32)
    rsu_all = ab("rsu_all", [128, IC], F32)
    rsd_all = ab("rsd_all", [128, 8], F32)

    v_aug4 = lambda: v_aug[:].rearrange("p (s h c) -> p s h c", s=NKS, h=2)

    stA = ExitStack()

    def alloc_A():
        def sa(name, shape, dt):
            return stA.enter_context(nc.sbuf_tensor(name, shape, dt))
        o = {}
        o["wcv"] = sa("wcv", [128, 8 * 3072], BF16)
        o["wkc"] = sa("wkc", [128, 8 * 128], BF16)
        o["wvc"] = sa("wvc", [128, 8 * 128], BF16)
        o["wqc"] = sa("wqc", [128, 8 * 128], BF16)
        o["wkcT"] = sa("wkcT", [128, 24 * 128], BF16)
        o["wvcT"] = sa("wvcT", [128, 24 * 128], BF16)
        o["xw"] = sa("xw", [128, 2 * 8 * SEGW], BF16)
        o["xw3"] = sa("xw3", [128, 2 * 3 * 8 * 130], BF16)
        o["x0b"] = sa("x0b", [128, 8], BF16)
        o["v0c"] = sa("v0c", [128, 2], F32)
        o["mones"] = sa("mones", [128, 384], BF16)
        o["wnat"] = sa("wnat", [128, 1408], F32)
        o["absw"] = sa("absw", [128, 2], F32)
        o["ps_w"] = stA.enter_context(nc.psum_tensor("ps_w", [128, 128], F32))
        o["ps_k"] = stA.enter_context(nc.psum_tensor("ps_k", [128, 1024], F32))
        o["ps_v"] = stA.enter_context(nc.psum_tensor("ps_v", [128, 1024], F32))
        o["ps_q"] = stA.enter_context(nc.psum_tensor("ps_q", [128, 1024], F32))
        return o

    def A_gpsimd(g, A):
        ctx.gdma(g, out=A["wcv"][:].rearrange("p (c m) -> p c m", c=8),
                 in_=bass.AP(wconvN, 0, [[3072, 128], [128 * 3072, 8], [1, 3072]]))
        for nsb, wh in (("wkc", wkTc), ("wvc", wvTc), ("wqc", wqTc)):
            ctx.gdma(g, out=A[nsb][:].rearrange("p (c m) -> p c m", c=8),
                     in_=bass.AP(wh, 0, [[128, 128], [128 * 128, 8], [1, 128]]))
        ctx.gdma(g, out=A["x0b"][:].rearrange("p (c u) -> p c u", u=1),
                 in_=bass.AP(xT, 0, [[T, 128], [128 * T, 8], [1, 1]]),
                 allow_slow_non_contiguous=True)
        ctx.gdma(g, out=cid_sb[:], in_=cid.ap())
        ctx.gdma(g, out=coff_sb[:], in_=coff.ap())
        ctx.mark("A.w", "lg")
        inst = g.memset(A["mones"][:], 1.0)
        ctx.inc(inst, "gm", 1)
        ctx.selfwait(g, "gm")
        inst = g.affine_select(m384[:], A["mones"][:], pattern=[[1, 384]],
                               compare_op=ALU.is_ge, fill=0.0, base=0,
                               channel_multiplier=-3)
        ctx.inc(inst, "gm", 1)
        inst = g.memset(v_aug4()[:, :, :, 64:128], 1.0)
        ctx.inc(inst, "gm", 1)
        ctx.mark("A.ginit", "gm")
        xw4 = A["xw"][:].rearrange("p (b c m) -> p b c m", b=2, c=8)
        for s in range(NSEG):
            a, n, doff = seg_cols(s)
            if s >= 2:
                ctx.wait(g, f"A.v.x3{s - 2}")
                ctx.wait(g, f"A.t.seg{s - 2}q")
            if doff > 0:
                g.memset(xw4[:, s % 2, :, 0:doff], 0.0)
            if doff + n < SEGW:
                g.memset(xw4[:, s % 2, :, doff + n:SEGW], 0.0)
            ctx.gdma(g, out=xw4[:, s % 2, :, doff:doff + n],
                     in_=bass.AP(xT, a, [[T, 128], [128 * T, 8], [1, n]]))
            ctx.mark(f"A.seg{s}", "lg")
        ctx.wait(g, "A.v0row")
        ctx.wait(g, "A.v.seg0v")
        for h in range(2):
            ctx.gdma(g, out=v_aug[0:1, h * 128:h * 128 + 64],
                     in_=bass.AP(v0_d, 64 * h, [[1, 1], [1, 64]]))
        ctx.mark("A.v0aug", "lg")
        ctx.selfwait(g, "lg")
        ctx.selfwait(g, "gm")
        ctx.barrier_inc(g)

    def A_sync(sy, A):
        for qi, (w, ic) in enumerate(QSC):
            if qi >= 1:
                ctx.wait(sy, f"A.v.qn{qi - 1}")
            if w == "d":
                ctx.sdma(sy, key=f"A.nat{qi}", out=A["wnat"][:, 0:1408],
                         in_=bass.AP(wdN, ic * 128 * DI, [[DI, 128], [1, 1408]]))
                ctx.wait(sy, f"A.v.qd{qi}")
                ctx.sdma(sy, key=f"A.nat{qi}b", out=A["wnat"][:, 0:1408],
                         in_=bass.AP(wdN, ic * 128 * DI + 1408, [[DI, 128], [1, 1408]]))
            else:
                src = bass.AP(wgN if w == "g" else wuN, ic * 128 * D, [[D, 128], [1, 1024]])
                ctx.sdma(sy, key=f"A.nat{qi}", out=A["wnat"][:, 0:1024], in_=src)
        for qi, (w, ic) in enumerate(QSC):
            ctx.wait(sy, f"A.v.qn{qi}")
            ral = {"g": rsg_all, "u": rsu_all, "d": rsd_all}[w]
            base = {"g": 0, "u": DI, "d": 2 * DI}[w]
            ctx.sdma(sy, out=bass.AP(rs_d, base + ic * 128, [[1, 128], [1, 1]]),
                     in_=ral[:, ic:ic + 1])
        ctx.wait(sy, "A.v.v0c")
        ctx.sdma(sy, key="A.v0row", out=bass.AP(v0_d, 0, [[1, 128], [1, 1]]), in_=A["v0c"][:, 0:1])
        ctx.sdrain(sy)
        self_inc = sy.sem_inc(ctx.sems["sy"], 1) if not ctx.mock else None
        ctx.cnt["sy"] = ctx.cnt.get("sy", 0) + 1
        ctx.mark("A.rsd", "sy")
        ctx.barrier_inc(sy)

    def A_tensor(t, A):
        ctx.wait(t, "A.w")
        ctx.wait(t, "A.v.ones")
        wcv4 = A["wcv"][:].rearrange("p (c m) -> p c m", c=8)
        widx = 0
        for tag, wsl, dst in (("k", A["wkc"], A["wkcT"]), ("v", A["wvc"], A["wvcT"])):
            for mt in range(24):
                if widx >= 1:
                    pk, pm = ("k", mt - 1) if tag == "k" else (("k", 23) if mt == 0 else ("v", mt - 1))
                    ctx.wait(t, f"A.v.w{pk}{pm}")
                inst = None
                for icb in range(8):
                    inst = t.matmul(A["ps_w"][:], wcv4[:, icb, mt * 128:mt * 128 + 128],
                                    wsl[:, icb * 128:icb * 128 + 128],
                                    start=(icb == 0), stop=(icb == 7))
                ctx.inc(inst, "tm", 1)
                ctx.mark(f"A.t.w{tag}{mt}", "tm")
                widx += 1
        ctx.wait(t, "A.v.wv23")
        inst = None
        for icb in range(8):
            inst = t.matmul(A["ps_w"][:, 0:1], A["wkc"][:, icb * 128:icb * 128 + 128],
                            A["x0b"][:, icb:icb + 1], start=(icb == 0), stop=(icb == 7))
        ctx.inc(inst, "tm", 1)
        ctx.mark("A.t.k0", "tm")
        ctx.wait(t, "A.v.k0e")
        for icb in range(8):
            inst = t.matmul(A["ps_w"][:, 0:1], A["wvc"][:, icb * 128:icb * 128 + 128],
                            A["x0b"][:, icb:icb + 1], start=(icb == 0), stop=(icb == 7))
        ctx.inc(inst, "tm", 1)
        ctx.mark("A.t.v0", "tm")
        for s in range(NSEG):
            ctx.wait(t, f"A.v.x3{s}")
            if s >= 2:
                ctx.wait(t, f"A.sc.k{s - 2}")
                ctx.wait(t, f"A.v.seg{s - 2}v")
                ctx.wait(t, f"A.sc.q{s - 2}")
            else:
                ctx.wait(t, "A.v.k0v0")
            x3b = (s % 2) * 3 * 8 * 130
            inst = None
            for mc in range(24):
                w, dinc = mc // 8, mc % 8
                a0 = x3b + (w * 8 + dinc) * 130
                inst = t.matmul(A["ps_k"][:, (s % 2) * 512:(s % 2) * 512 + 128],
                                A["wkcT"][:, mc * 128:mc * 128 + 128],
                                A["xw3"][:, a0 + 1:a0 + 129],
                                start=(mc == 0), stop=(mc == 23))
            ctx.inc(inst, "tm", 1)
            ctx.mark(f"A.t.seg{s}k", "tm")
            for mc in range(24):
                w, dinc = mc // 8, mc % 8
                a0 = x3b + (w * 8 + dinc) * 130
                inst = t.matmul(A["ps_v"][:, (s % 2) * 512:(s % 2) * 512 + 128],
                                A["xw3"][:, a0:a0 + 128],
                                A["wvcT"][:, mc * 128:mc * 128 + 128],
                                start=(mc == 0), stop=(mc == 23))
            ctx.inc(inst, "tm", 1)
            ctx.mark(f"A.t.seg{s}v", "tm")
            nq = seg_n_q(s)
            ctx.wait(t, f"A.seg{s}")
            for icb in range(8):
                a0 = (s % 2) * 8 * SEGW + icb * SEGW + 3
                inst = t.matmul(A["ps_q"][:, (s % 2) * 512:(s % 2) * 512 + nq],
                                A["wqc"][:, icb * 128:icb * 128 + 128],
                                A["xw"][:, a0:a0 + nq],
                                start=(icb == 0), stop=(icb == 7))
            ctx.inc(inst, "tm", 1)
            ctx.mark(f"A.t.seg{s}q", "tm")
            ctx.mark(f"A.t.seg{s}", "tm")
        ctx.selfwait(t, "tm")
        ctx.barrier_inc(t)

    def A_vector(v, A):
        v.memset(ones128[:], 1.0)
        v.memset(magic_sb[:], MAGIC)
        inst = v.memset(rmax[:], 0.0)
        ctx.inc(inst, "vm", 1)
        ctx.mark("A.v.ones", "vm")
        for tag, dst in (("k", A["wkcT"]), ("v", A["wvcT"])):
            for mt in range(24):
                ctx.wait(v, f"A.t.w{tag}{mt}")
                inst = v.tensor_copy(dst[:, mt * 128:mt * 128 + 128], A["ps_w"][:])
                ctx.inc(inst, "vm", 1)
                ctx.mark(f"A.v.w{tag}{mt}", "vm")
        ctx.wait(v, "A.t.k0")
        inst = v.tensor_copy(k_h[:, 0:1], A["ps_w"][:, 0:1])
        ctx.inc(inst, "vm", 1)
        ctx.mark("A.v.k0e", "vm")
        ctx.wait(v, "A.t.v0")
        inst = v.tensor_copy(A["v0c"][:, 0:1], A["ps_w"][:, 0:1])
        ctx.inc(inst, "vm", 1)
        ctx.mark("A.v.v0c", "vm")
        ctx.mark("A.v.k0v0", "vm")
        def emit_qsc(qi):
            w, ic = QSC[qi]
            sal = {"g": sg_all, "u": su_all, "d": sd_all}[w]
            ral = {"g": rsg_all, "u": rsu_all, "d": rsd_all}[w]
            ctx.wait(v, f"A.nat{qi}")
            if w == "d":
                inst = v.reduce_max(A["absw"][:, 0:1], A["wnat"][:, 0:1408],
                                    axis=AX.X, apply_absolute_value=True)
                ctx.inc(inst, "vm", 1)
                ctx.mark(f"A.v.qd{qi}", "vm")
                ctx.wait(v, f"A.nat{qi}b")
                inst = v.reduce_max(A["absw"][:, 1:2], A["wnat"][:, 0:1408],
                                    axis=AX.X, apply_absolute_value=True)
                ctx.inc(inst, "vm", 1)
                ctx.selfwait(v, "vm")
                inst = v.tensor_tensor(A["absw"][:, 0:1], A["absw"][:, 0:1],
                                       A["absw"][:, 1:2], ALU.max)
                ctx.inc(inst, "vm", 1)
                ctx.selfwait(v, "vm")
            else:
                inst = v.reduce_max(A["absw"][:, 0:1], A["wnat"][:, 0:1024],
                                    axis=AX.X, apply_absolute_value=True)
                ctx.inc(inst, "vm", 1)
                ctx.selfwait(v, "vm")
            inst = v.tensor_scalar_mul(sal[:, ic:ic + 1], A["absw"][:, 0:1], 1.0 / QMAX)
            ctx.inc(inst, "vm", 1)
            ctx.selfwait(v, "vm")
            inst = v.reciprocal(ral[:, ic:ic + 1], sal[:, ic:ic + 1])
            ctx.inc(inst, "vm", 1)
            ctx.mark(f"A.v.qn{qi}", "vm")

        def emit_destride(u):
            ctx.wait(v, f"A.seg{u}")
            if u >= 2:
                ctx.wait(v, f"A.t.seg{u - 2}v")
            x3 = A["xw3"][:].rearrange("p (b w c j) -> p b w c j", b=2, w=3, c=8)
            for w in range(3):
                inst = v.tensor_copy(
                    x3[:, u % 2, w, :, 0:129],
                    A["xw"][:].rearrange("p (b c m) -> p b c m", b=2, c=8)
                    [:, u % 2, :, w:w + 385:3])
                ctx.inc(inst, "vm", 1)
            ctx.mark(f"A.v.x3{u}", "vm")

        emit_destride(0)
        for s in range(NSEG):
            if s + 1 < NSEG:
                emit_destride(s + 1)
            ctx.wait(v, f"A.t.seg{s}v")
            inst = v.tensor_copy(v_aug4()[:, s, :, 0:64],
                                 A["ps_v"][:, (s % 2) * 512:(s % 2) * 512 + 128]
                                 .rearrange("p (h c) -> p h c", h=2))
            ctx.inc(inst, "vm", 1)
            ctx.mark(f"A.v.seg{s}v", "vm")
            for qi in range(len(QSC)):
                if QSEG[qi] == s:
                    emit_qsc(qi)
        ctx.selfwait(v, "vm")
        ctx.mark("A.v.scales", "vm")
        ctx.barrier_inc(v)

    def A_scalar(s_, A):
        for s in range(NSEG):
            ctx.wait(s_, f"A.t.seg{s}k")
            inst = s_.activation(k_h[:, KS * s + 1:KS * s + 129],
                                 A["ps_k"][:, (s % 2) * 512:(s % 2) * 512 + 128],
                                 AF.Copy)
            ctx.inc(inst, "sm", 1)
            ctx.mark(f"A.sc.k{s}", "sm")
            ctx.wait(s_, f"A.t.seg{s}q")
            nq = seg_n_q(s)
            inst = s_.activation(q_h[:, 384 * s:384 * s + nq],
                                 A["ps_q"][:, (s % 2) * 512:(s % 2) * 512 + nq],
                                 AF.Copy)
            ctx.inc(inst, "sm", 1)
            ctx.mark(f"A.sc.q{s}", "sm")
        ctx.selfwait(s_, "sm")
        ctx.barrier_inc(s_)

    stB = ExitStack()

    def alloc_B():
        def sbb(name, shape, dt):
            return stB.enter_context(nc.sbuf_tensor(name, shape, dt))
        o = {}
        o["pt"] = sbb("pt", [128, 6 * 2048], BF16)
        o["oh"] = sbb("oh", [128, 1024], F32)
        o["dd"] = sbb("dd", [128, 1024], F32)
        o["bd"] = sbb("bd", [128, 1024], F32)
        o["ones1k"] = sbb("ones1k", [128, 1024], F32)
        o["ot"] = sbb("ot", [128, 4 * 1024], BF16)
        o["ps_s"] = stB.enter_context(nc.psum_tensor("ps_s", [128, 2048], F32))
        o["ps_oA"] = stB.enter_context(nc.psum_tensor("ps_oA", [128, 1024], F32))
        o["ps_oB"] = stB.enter_context(nc.psum_tensor("ps_oB", [128, 1024], F32))
        return o

    def B_tensor(t, B):
        ps_s, ps_oA, ps_oB, pt = B["ps_s"], B["ps_oA"], B["ps_oB"], B["pt"]
        ctx.wait(t, f"A.sc.k{NSEG - 1}")
        ctx.wait(t, f"A.sc.q{NSEG - 1}")
        ctx.wait(t, f"A.v.seg{NSEG - 1}v")
        ctx.wait(t, "A.v0aug")
        ctx.wait(t, "A.ginit")
        ck = [0]

        def emit_pv(ti, i):
            s = SUBS[ti][i]
            g = GID[(ti, i)]
            if mask_pieces(ti, s):
                ctx.wait(t, f"B.v.mk{g}")
            else:
                ctx.wait(t, f"B.s.ex{g}")
            if i == 0 and ti != TORDER[0]:
                ctx.wait(t, f"B.v.oc{PREV_T[ti]}")
            inst = None
            for h, pso in ((0, ps_oA), (1, ps_oB)):
                for c in range(pv_c0(ti, s), 2):
                    qs = max(512 * c, qx_of(ti, s), 0)
                    n = 512 * (c + 1) - qs
                    base = (g % 6) * 2048 + off(c, h) + (qs - 512 * c)
                    inst = t.matmul(pso[:, qs:qs + n],
                                    v_aug[:, s * 256 + h * 128:s * 256 + h * 128 + 128],
                                    pt[:, base:base + n],
                                    start=(i == 0), stop=(s == last_sub(ti, c)))
            ctx.inc(inst, "tm", 1)
            ctx.mark(f"B.t.pv{g}", "tm")

        for tp, ti in enumerate(TORDER):
            for i, s in enumerate(SUBS[ti]):
                g = GID[(ti, i)]
                ql = ql_of(ti, s)
                for c in range(ql // 512, 2):
                    kc = ck[0]; ck[0] += 1
                    if kc >= 2:
                        ctx.wait(t, f"B.s.exc{kc - 2}")
                    inst = None
                    for h in range(2):
                        inst = t.matmul(ps_s[:, (kc % 2) * 1024 + h * 512:(kc % 2) * 1024 + h * 512 + 512],
                                        k_h[64 * h:64 * h + 64, KS * s:KS * s + KS],
                                        q_h[64 * h:64 * h + 64, TQ * ti + 512 * c:TQ * ti + 512 * c + 512],
                                        start=True, stop=True, tile_position=(64 * h, 0))
                    ctx.inc(inst, "tm", 1)
                    ctx.mark(f"B.t.mmc{kc}", "tm")
                ctx.mark(f"B.t.mm{g}", "tm")
                if i > 0:
                    emit_pv(ti, i - 1)
            emit_pv(ti, len(SUBS[ti]) - 1)
        ctx.selfwait(t, "tm")
        ctx.barrier_inc(t)

    def B_scalar(s_, B):
        pt, ps_s = B["pt"], B["ps_s"]
        ctx.wait(s_, "A.v.scales")
        ck = [0]
        inst = None
        for tp, ti in enumerate(TORDER):
            for i, s in enumerate(SUBS[ti]):
                g = GID[(ti, i)]
                if g >= 7:
                    ctx.wait(s_, f"B.t.pv{g - 6}")
                ql = ql_of(ti, s)
                if ql == 0 and ck[0] % 2 == 0:
                    kc = ck[0]; ck[0] += 2
                    ctx.wait(s_, f"B.t.mmc{kc + 1}")
                    inst = s_.activation(pt[:, (g % 6) * 2048:(g % 6) * 2048 + 2048],
                                         ps_s[:], AF.Exp, scale=0.125)
                    ctx.inc(inst, "sm", 1)
                    ctx.mark(f"B.s.exc{kc}", "sm")
                    ctx.mark(f"B.s.exc{kc + 1}", "sm")
                else:
                    for c in range(ql // 512, 2):
                        kc = ck[0]; ck[0] += 1
                        ctx.wait(s_, f"B.t.mmc{kc}")
                        inst = s_.activation(pt[:, (g % 6) * 2048 + off(c, 0):(g % 6) * 2048 + off(c, 0) + 1024],
                                             ps_s[:, (kc % 2) * 1024:(kc % 2) * 1024 + 1024],
                                             AF.Exp, scale=0.125)
                        ctx.inc(inst, "sm", 1)
                        ctx.mark(f"B.s.exc{kc}", "sm")
                ctx.mark(f"B.s.ex{g}", "sm")
        ctx.selfwait(s_, "sm")
        ctx.barrier_inc(s_)

    def B_vector(v, B):
        pt, ot, oh = B["pt"], B["ot"], B["oh"]
        dd, bd, ones1k = B["dd"], B["bd"], B["ones1k"]
        ps_oA, ps_oB = B["ps_oA"], B["ps_oB"]
        ctx.barrier_wait(v, 5)
        inst = v.memset(ones1k[:], 1.0)
        ctx.inc(inst, "vm", 1)
        ctx.selfwait(v, "vm")

        def normalize(tpv):
            tiv = TORDER[tpv]
            if tpv >= 4:
                ctx.wait(v, f"B.ot{TORDER[tpv - 4]}")
            ctx.selfwait(v, "vm")
            for c0 in range(0, 1024, 256):
                inst = v.reciprocal(bd[:, c0:c0 + 256], dd[:, c0:c0 + 256])
                ctx.inc(inst, "vm", 1)
            ctx.selfwait(v, "vm")
            inst = v.tensor_tensor(ot[0:64, (tpv % 4) * 1024:(tpv % 4) * 1024 + 1024],
                                   oh[0:64, :], bd[0:64, :], ALU.mult)
            ctx.inc(inst, "vm", 1)
            inst = v.tensor_tensor(ot[64:128, (tpv % 4) * 1024:(tpv % 4) * 1024 + 1024],
                                   oh[64:128, :], bd[64:128, :], ALU.mult)
            ctx.inc(inst, "vm", 1)
            ctx.mark(f"B.v.den{tiv}", "vm")

        def quant(tp):
            for qi in QB[tp]:
                buf = qi % 4
                ctx.wait(v, f"B.qa{qi}")
                ctx.wait(v, f"B.qb{qi}")
                if qi >= 6:
                    ctx.wait(v, f"B.qst{qi - 6}")
                wv_ = wtile[:, buf * 1024:buf * 1024 + 1024]
                rv_ = rs_bc[:, buf * 1024:buf * 1024 + 1024]
                bv_ = wqb[:, (qi % 6) * 1024:(qi % 6) * 1024 + 1024]
                inst = v.tensor_mul(rv_, wv_, rv_)
                ctx.inc(inst, "vm", 1)
                ctx.selfwait(v, "vm")
                ctx.mark(f"B.g.q{qi}", "vm")
                inst = v.tensor_scalar(bv_, rv_, MAGIC, MAGIC, ALU.add, ALU.subtract)
                ctx.inc(inst, "vm", 1)
                ctx.selfwait(v, "vm")
                ctx.mark(f"B.g.qb{qi}", "vm")

        for tp, ti in enumerate(TORDER):
            for i, s in enumerate(SUBS[ti]):
                g = GID[(ti, i)]
                pieces = mask_pieces(ti, s)
                if pieces:
                    ctx.wait(v, f"B.s.ex{g}")
                    for (h, a, b) in pieces:
                        c = a // 512
                        base = (g % 6) * 2048 + off(c, h) + (a - 512 * c)
                        q0 = qx_of(ti, s)
                        inst = v.tensor_mul(pt[:, base:base + (b - a)],
                                            pt[:, base:base + (b - a)],
                                            m384[:, a - q0:b - q0])
                        ctx.inc(inst, "vm", 1)
                    ctx.mark(f"B.v.mk{g}", "vm")
                if i == min(3, len(SUBS[ti]) - 1):
                    if tp > 0:
                        normalize(tp - 1)
                    quant(tp)
            gl = GID[(ti, len(SUBS[ti]) - 1)]
            ctx.wait(v, f"B.t.pv{gl}")
            ctx.selfwait(v, "vm")
            inst = v.tensor_tensor(oh[0:64, :], ps_oA[0:64, :],
                                   ones1k[0:64, :], ALU.mult)
            ctx.inc(inst, "vm", 1)
            inst = v.tensor_tensor(oh[64:128, :], ps_oB[0:64, :],
                                   ones1k[64:128, :], ALU.mult)
            ctx.inc(inst, "vm", 1)
            inst = v.tensor_tensor(dd[0:64, :], ps_oA[64:128, :],
                                   ones1k[0:64, :], ALU.mult)
            ctx.inc(inst, "vm", 1)
            inst = v.tensor_tensor(dd[64:128, :], ps_oB[64:128, :],
                                   ones1k[64:128, :], ALU.mult)
            ctx.inc(inst, "vm", 1)
            ctx.mark(f"B.v.oc{ti}", "vm")
        normalize(NT - 1)
        ctx.selfwait(v, "vm")
        ctx.barrier_inc(v)

    def B_sync(sy, B):
        ot = B["ot"]
        ctx.barrier_wait(sy, 5)
        ctx.wait(sy, "A.rsd")
        LQ = [[] for _ in range(NT)]
        for _t in range(NT):
            for _q in QB[_t]:
                LQ[max(0, _t - 2)].append(_q)
        for tp, ti in enumerate(TORDER):
            for qi in LQ[tp]:
                w, ic = QIT[qi]
                buf = qi % 4
                if qi >= 4:
                    ctx.wait(sy, f"B.g.qb{qi - 4}")
                if w == "d":
                    src = bass.AP(wdT, ic * 128 * D, [[D, 128], [128, 8], [1, 128]])
                    rsrc = bass.AP(rs_d, 2 * DI, [[0, 128], [128, 8], [1, 128]])
                else:
                    tt = wgT if w == "g" else wuT
                    src = bass.AP(tt, ic * 128, [[DI, 128], [128 * DI, 8], [1, 128]])
                    rsrc = bass.AP(rs_d, (0 if w == "g" else DI) + ic * 128,
                                   [[0, 128], [0, 8], [1, 128]])
                ctx.sdma(sy, key=f"B.qa{qi}",
                         out=wtile[:].rearrange("p (b hc j) -> p b hc j", b=4, hc=8)[:, buf],
                         in_=src)
                ctx.sdma(sy, key=f"B.qb{qi}",
                         out=rs_bc[:].rearrange("p (b hc j) -> p b hc j", b=4, hc=8)[:, buf],
                         in_=rsrc)

            ctx.wait(sy, f"B.v.den{ti}")
            ctx.sdma(sy, key=f"B.ot{ti}",
                     out=bass.AP(o_in, (ti // 2) * 128 * (T // 8) + (ti % 2) * TQ,
                                 [[T // 8, 128], [1, TQ]]),
                     in_=ot[:, (tp % 4) * 1024:(tp % 4) * 1024 + 1024])
            if tp % 2 == 1:
                ctx.sdrain(sy)
                if not ctx.mock:
                    sy.sem_inc(ctx.sems["sy"], 1)
                ctx.cnt["sy"] = ctx.cnt.get("sy", 0) + 1
                ctx.mark(f"B.otq{ti // 2}", "sy")
        ctx.barrier_inc(sy)

    def B_gpsimd(g, B):
        ctx.barrier_wait(g, 5)
        for tp, ti in enumerate(TORDER):
            for qi in QB[tp]:
                w, ic = QIT[qi]
                ctx.wait(g, f"B.g.qb{qi}")
                inst = ctx.gdma(g, out=bass.AP(wq_d[w], ic * 8 * 128 * 128,
                                               [[128, 128], [128 * 128, 8], [1, 128]]),
                                in_=wqb[:].rearrange("p (b hc j) -> p b hc j", b=6, hc=8)[:, qi % 6])
                ctx.mark(f"B.qst{qi}", "lg")
            if tp % 2 == 1:
                q = ti // 2
                ctx.wait(g, f"B.otq{q}")
                inst = g.collective_compute("AllGather", ALU.bypass,
                                            replica_groups=[list(range(NCORE))],
                                            ins=[o_in[q]], outs=[o_ag[q]])
                ctx.inc(inst, "gm", 1)
        ctx.selfwait(g, "gm")
        ctx.barrier_inc(g)

    stC1 = ExitStack()

    def alloc_C1():
        def sc1(name, shape, dt):
            return stC1.enter_context(nc.sbuf_tensor(name, shape, dt))
        o = {}
        o["o_c"] = sc1("o_c", [128, 8 * 2048], BF16)
        o["woT_sb"] = sc1("woT_sb", [128, 8 * 1024], BF16)
        o["bo_sb"] = sc1("bo_sb", [128, 8], F32)
        o["xrs"] = sc1("xrs", [128, 8 * 512], F32)
        o["t1a"] = sc1("t1a", [128, 2 * 512], F32)
        o["t1"] = sc1("t1", [128, 8 * 512], F32)
        o["t1sq"] = sc1("t1sq", [128, 8 * 512], F32)
        o["mbc"] = sc1("mbc", [128, 512], F32)
        o["m2bc"] = sc1("m2bc", [128, 512], F32)
        o["varb"] = sc1("varb", [128, 512], F32)
        o["sdev"] = sc1("sdev", [128, 512], F32)
        o["invb"] = sc1("invb", [128, 512], F32)
        o["xln"] = sc1("xln", [128, 8 * 512], F32)
        o["xq"] = sc1("xq", [128, 8 * 2048], BF16)
        o["wf"] = sc1("wf", [128, 2 * 2048], BF16)
        o["hsil"] = sc1("hsil", [128, 2 * 512], F32)
        o["ht"] = sc1("ht", [128, 2 * 512], F32)
        o["habs"] = sc1("habs", [128, 2], F32)
        o["ps_at"] = stC1.enter_context(nc.psum_tensor("ps_at", [128, 1024], F32))
        o["ps_M"] = stC1.enter_context(nc.psum_tensor("ps_M", [128, 512], F32))
        o["ps_SQ"] = stC1.enter_context(nc.psum_tensor("ps_SQ", [128, 512], F32))
        o["ps_g"] = stC1.enter_context(nc.psum_tensor("ps_g", [128, 1024], F32))
        o["ps_u"] = stC1.enter_context(nc.psum_tensor("ps_u", [128, 1024], F32))
        return o

    def C1_gpsimd(g, C):
        ctx.barrier_wait(g, 10)
        ctx.gdma(g, out=C["woT_sb"][:].rearrange("p (hc j) -> p hc j", hc=8),
                 in_=bass.AP(woT, 0, [[D, 128], [128 * D, 8], [1, D]]))
        ctx.mark("C1.woT", "lg")
        ctx.gdma(g, out=scal[:, 12:14], in_=bass.AP(clipx, 0, [[0, 128], [0, 2], [1, 1]]))
        ctx.gdma(g, out=scal[:, 14:16], in_=bass.AP(clipd, 0, [[0, 128], [0, 2], [1, 1]]))
        ctx.mark("C1.clipraw", "lg")
        ctx.wait(g, "C1.ar1in")
        inst = g.collective_compute("AllReduce", ALU.max,
                                    replica_groups=[list(range(NCORE))],
                                    ins=[ar1_in.ap()], outs=[ar1_out.ap()])
        ctx.inc(inst, "gm", 1)
        ctx.selfwait(g, "gm")
        ctx.gdma(g, out=rrow[0:1, 0:128], in_=bass.AP(ar1_out, 0, [[1, 1], [1, 128]]))
        ctx.mark("C1.ar1row", "lg")
        ctx.wait(g, "C1.gmaxb")
        ctx.gdma(g, out=scal[:, 0:2], in_=bass.AP(sc_d, 0, [[0, 128], [0, 2], [1, 1]]))
        ctx.mark("C1.gmaxbc", "lg")
        ctx.selfwait(g, "lg")
        ctx.barrier_inc(g)

    def C1_sync(sy, C):
        ctx.barrier_wait(sy, 10)
        for c2 in range(4):
            if not ctx.mock:
                off_d = sy.value_load(coff_sb[0:1, c2:c2 + 1]) * 1
            else:
                off_d = 0
            ctx.sdma(sy, key=f"C1.oc{c2}",
                     out=C["o_c"][:].rearrange("p (r j) -> p r j", r=8)[:, 2 * c2:2 * c2 + 2],
                     in_=bass.AP(o_ag, off_d, [[T // 8, 128], [128 * (T // 8), 2], [1, TPC]]))
        ctx.sdma(sy, key="C1.bo", out=C["bo_sb"][:],
                 in_=bass.AP(bo, 0, [[1, 128], [128, 8]]),
                 allow_slow_non_contiguous=True)
        for gs in range(4):
            if gs > 0:
                ctx.wait(sy, f"C1.v.t1all{gs - 1}")
            ctx.sdma(sy, key=f"C1.xr{gs}",
                     out=C["xrs"][:].rearrange("p (dc j) -> p dc j", dc=8),
                     in_=bass.AP(xres, 512 * gs, [[TPC, 128], [128 * TPC, 8], [1, 512]]))
            ctx.wait(sy, f"C1.v.xln{gs}")
            ctx.sdma(sy, key=f"C1.xlnst{gs}",
                     out=bass.AP(xln_d, 512 * gs, [[TPC, 128], [128 * TPC, 8], [1, 512]]),
                     in_=C["xln"][:].rearrange("p (dc q) -> p dc q", dc=8))
        ctx.wait(sy, "C1.v.xmax")
        ctx.sdma(sy, key="C1.ar1in", out=bass.AP(ar1_in, 0, [[1, 128], [1, 1]]),
                 in_=rmax[:, 0:1])
        ctx.wait(sy, "C1.v.gmaxred")
        ctx.sdma(sy, key="C1.gmaxb", out=bass.AP(sc_d, 0, [[1, 1], [1, 1]]),
                 in_=rrow[0:1, 128:129])
        for gs in range(4):
            if gs == 0:
                ctx.wait(sy, "C1.xlnst3")
            else:
                ctx.wait(sy, f"C1.s.xq{gs - 1}")
            ctx.wait(sy, f"C1.xlnst{gs}")
            ctx.sdma(sy, key=f"C1.xqld{gs}",
                     out=C["xln"][:].rearrange("p (dc q) -> p dc q", dc=8),
                     in_=bass.AP(xln_d, 512 * gs, [[TPC, 128], [128 * TPC, 8], [1, 512]]))
        for ic in range(IC + 1):
            if ic < IC:
                if ic >= 2:
                    ctx.wait(sy, f"C1.t.guic{ic - 2}")
                for wi, wdsrc in ((0, wq_d["g"]), (1, wq_d["u"])):
                    ctx.sdma(sy, key=f"C1.gw{ic}{'a' if wi == 0 else 'b'}",
                             out=C["wf"][:, (ic % 2) * 2048 + wi * 1024:(ic % 2) * 2048 + wi * 1024 + 1024]
                             .rearrange("p (hc j) -> p hc j", hc=8),
                             in_=bass.AP(wdsrc, ic * 8 * 128 * 128,
                                         [[128, 128], [128 * 128, 8], [1, 128]]))
            if ic >= 1:
                for gs in range(4):
                    kk = (ic - 1) * 4 + gs
                    ctx.wait(sy, f"C1.v.h{kk}")
                    ctx.sdma(sy, key=f"C1.hst{kk}",
                             out=bass.AP(h_d, (ic - 1) * 128 * TPC + 512 * gs, [[TPC, 128], [1, 512]]),
                             in_=C["ht"][:, (kk % 2) * 512:(kk % 2) * 512 + 512])
        ctx.wait(sy, "C1.v.hmax")
        ctx.sdma(sy, key="C1.ar2in", out=bass.AP(ar2_in, 0, [[1, 128], [1, 1]]),
                 in_=rmax[:, 1:2])
        ctx.sdrain(sy)
        if not ctx.mock:
            sy.sem_inc(ctx.sems["sy"], 1)
        ctx.cnt["sy"] = ctx.cnt.get("sy", 0) + 1
        ctx.mark("C1.alldma", "sy")
        ctx.barrier_inc(sy)

    def C1_tensor(t, C):
        ctx.barrier_wait(t, 10)
        ctx.wait(t, "C1.woT")
        for _c2 in range(4):
            ctx.wait(t, f"C1.oc{_c2}")
        for gs in range(4):
            for dc in range(8):
                k = gs * 8 + dc
                if k >= 2:
                    ctx.wait(t, f"C1.s.bo{k - 2}")
                inst = None
                for hc in range(8):
                    inst = t.matmul(C["ps_at"][:, (k % 2) * 512:(k % 2) * 512 + 512],
                                    C["woT_sb"][:, hc * 1024 + dc * 128:hc * 1024 + dc * 128 + 128],
                                    C["o_c"][:, hc * 2048 + 512 * gs:hc * 2048 + 512 * gs + 512],
                                    start=(hc == 0), stop=(hc == 7))
                ctx.inc(inst, "tm", 1)
                ctx.mark(f"C1.t.at{k}", "tm")
            ctx.wait(t, f"C1.v.t1all{gs}")
            ctx.wait(t, f"C1.v.sqall{gs}")
            if gs > 0:
                ctx.wait(t, f"C1.v.var{gs - 1}")
            inst = None
            for dc in range(8):
                inst = t.matmul(C["ps_M"][:], ones128[:], C["t1"][:, dc * 512:dc * 512 + 512],
                                start=(dc == 0), stop=(dc == 7))
            for dc in range(8):
                inst = t.matmul(C["ps_SQ"][:], ones128[:], C["t1sq"][:, dc * 512:dc * 512 + 512],
                                start=(dc == 0), stop=(dc == 7))
            ctx.inc(inst, "tm", 1)
            ctx.mark(f"C1.t.st{gs}", "tm")
        for ic in range(IC):
            ctx.wait(t, f"C1.gw{ic}a")
            ctx.wait(t, f"C1.gw{ic}b")
            for gs in range(4):
                k = ic * 4 + gs
                ctx.wait(t, f"C1.v.xq{gs}")
                if k >= 4:
                    ctx.wait(t, f"C1.v.h{k - 4}")
                if k < 4:
                    ctx.wait(t, "C1.s.bo31")
                    ctx.wait(t, "C1.v.var3")
                gslot = [C["ps_g"][:, 0:512], C["ps_at"][:, 0:512],
                         C["ps_g"][:, 512:1024], C["ps_at"][:, 512:1024]][k % 4]
                uslot = [C["ps_u"][:, 0:512], C["ps_M"][:],
                         C["ps_u"][:, 512:1024], C["ps_SQ"][:]][k % 4]
                inst = None
                for hc in range(8):
                    inst = t.matmul(gslot,
                                    C["wf"][:, (ic % 2) * 2048 + hc * 128:(ic % 2) * 2048 + hc * 128 + 128],
                                    C["xq"][:, hc * 2048 + 512 * gs:hc * 2048 + 512 * gs + 512],
                                    start=(hc == 0), stop=(hc == 7))
                for hc in range(8):
                    inst = t.matmul(uslot,
                                    C["wf"][:, (ic % 2) * 2048 + 1024 + hc * 128:(ic % 2) * 2048 + 1024 + hc * 128 + 128],
                                    C["xq"][:, hc * 2048 + 512 * gs:hc * 2048 + 512 * gs + 512],
                                    start=(hc == 0), stop=(hc == 7))
                ctx.inc(inst, "tm", 1)
                ctx.mark(f"C1.t.gu{k}", "tm")
            ctx.mark(f"C1.t.guic{ic}", "tm")
        ctx.selfwait(t, "tm")
        ctx.barrier_inc(t)

    def C1_scalar(s_, C):
        ctx.barrier_wait(s_, 10)
        ctx.wait(s_, "C1.bo")
        for gs in range(4):
            for dc in range(8):
                k = gs * 8 + dc
                ctx.wait(s_, f"C1.t.at{k}")
                if k >= 2:
                    ctx.wait(s_, f"C1.v.t1_{k - 2}")
                inst = s_.activation(C["t1a"][:, (k % 2) * 512:(k % 2) * 512 + 512],
                                     C["ps_at"][:, (k % 2) * 512:(k % 2) * 512 + 512],
                                     AF.Identity, bias=C["bo_sb"][:, dc:dc + 1])
                ctx.inc(inst, "sm", 1)
                ctx.mark(f"C1.s.bo{k}", "sm")
            ctx.wait(s_, f"C1.v.var{gs}")
            inst = s_.activation(C["sdev"][:], C["varb"][:], AF.Sqrt)
            ctx.inc(inst, "sm", 1)
            ctx.mark(f"C1.s.sqrt{gs}", "sm")
        ctx.wait(s_, "C1.clipraw")
        inst = s_.activation(scal[:, 12:14], scal[:, 12:14], AF.Sigmoid)
        ctx.inc(inst, "sm", 1)
        ctx.selfwait(s_, "sm")
        inst = s_.activation(scal[:, 14:16], scal[:, 14:16], AF.Sigmoid)
        ctx.inc(inst, "sm", 1)
        ctx.mark("C1.s.sig", "sm")
        for gs in range(4):
            ctx.wait(s_, "C1.v.invs")
            ctx.wait(s_, f"C1.xqld{gs}")
            if gs > 0:
                ctx.wait(s_, f"C1.v.xq{gs - 1}")
            inst = None
            for dc in range(8):
                inst = s_.activation(C["t1sq"][:, dc * 512:dc * 512 + 512],
                                     C["xln"][:, dc * 512:dc * 512 + 512],
                                     AF.Identity, bias=magic_sb[:, 0:1], scale=scal[:, 4:5])
                ctx.inc(inst, "sm", 1)
            ctx.mark(f"C1.s.xq{gs}", "sm")
        for ic in range(IC):
            for gs in range(4):
                k = ic * 4 + gs
                ctx.wait(s_, f"C1.t.gu{k}")
                ctx.wait(s_, "C1.v.sgi")
                if k >= 2:
                    ctx.wait(s_, f"C1.v.h{k - 2}")
                gslot = [C["ps_g"][:, 0:512], C["ps_at"][:, 0:512],
                         C["ps_g"][:, 512:1024], C["ps_at"][:, 512:1024]][k % 4]
                inst = s_.activation(C["hsil"][:, (k % 2) * 512:(k % 2) * 512 + 512],
                                     gslot, AF.Silu, scale=sgi[:, ic:ic + 1])
                ctx.inc(inst, "sm", 1)
                ctx.mark(f"C1.s.si{k}", "sm")
        ctx.selfwait(s_, "sm")
        ctx.barrier_inc(s_)

    def C1_vector(v, C):
        ctx.barrier_wait(v, 10)
        for gs in range(4):
            ctx.wait(v, f"C1.xr{gs}")
            for dc in range(8):
                k = gs * 8 + dc
                ctx.wait(v, f"C1.s.bo{k}")
                inst = v.tensor_add(C["t1"][:, dc * 512:dc * 512 + 512],
                                    C["t1a"][:, (k % 2) * 512:(k % 2) * 512 + 512],
                                    C["xrs"][:, dc * 512:dc * 512 + 512])
                ctx.inc(inst, "vm", 1)
                ctx.mark(f"C1.v.t1_{k}", "vm")
                ctx.selfwait(v, "vm")
                inst = v.tensor_mul(C["t1sq"][:, dc * 512:dc * 512 + 512],
                                    C["t1"][:, dc * 512:dc * 512 + 512],
                                    C["t1"][:, dc * 512:dc * 512 + 512])
                ctx.inc(inst, "vm", 1)
            ctx.mark(f"C1.v.t1all{gs}", "vm")
            ctx.mark(f"C1.v.sqall{gs}", "vm")
            ctx.wait(v, f"C1.t.st{gs}")
            inst = v.tensor_scalar_mul(C["mbc"][:], C["ps_M"][:], 1.0 / D)
            ctx.inc(inst, "vm", 1)
            ctx.selfwait(v, "vm")
            inst = v.tensor_mul(C["m2bc"][:], C["mbc"][:], C["mbc"][:])
            ctx.inc(inst, "vm", 1)
            ctx.selfwait(v, "vm")
            inst = v.tensor_scalar_sub(C["m2bc"][:], C["m2bc"][:], 1e-5)
            ctx.inc(inst, "vm", 1)
            ctx.selfwait(v, "vm")
            inst = v.scalar_tensor_tensor(C["varb"][:], C["ps_SQ"][:], 1.0 / D, C["m2bc"][:],
                                          ALU.mult, ALU.subtract)
            ctx.inc(inst, "vm", 1)
            ctx.mark(f"C1.v.var{gs}", "vm")
            ctx.wait(v, f"C1.s.sqrt{gs}")
            for c0 in range(0, 512, 256):
                inst = v.reciprocal(C["invb"][:, c0:c0 + 256], C["sdev"][:, c0:c0 + 256])
                ctx.inc(inst, "vm", 1)
            ctx.selfwait(v, "vm")
            if gs > 0:
                ctx.wait(v, f"C1.xlnst{gs - 1}")
            for dc in range(8):
                inst = v.scalar_tensor_tensor(C["xln"][:, dc * 512:dc * 512 + 512],
                                              C["mbc"][:], -1.0, C["t1"][:, dc * 512:dc * 512 + 512],
                                              ALU.mult, ALU.add)
                ctx.inc(inst, "vm", 1)
            ctx.selfwait(v, "vm")
            for dc in range(8):
                xv = C["xln"][:, dc * 512:dc * 512 + 512]
                inst = v.tensor_mul(xv, xv, C["invb"][:])
                ctx.inc(inst, "vm", 1)
            ctx.mark(f"C1.v.xln{gs}", "vm")
            for dc in range(8):
                ctx.selfwait(v, "vm")
                inst = v.reduce_max(C["habs"][:, 0:1],
                                    C["xln"][:, dc * 512:dc * 512 + 512],
                                    axis=AX.X, apply_absolute_value=True)
                ctx.inc(inst, "vm", 1)
                ctx.selfwait(v, "vm")
                inst = v.tensor_max(rmax[:, 0:1], rmax[:, 0:1], C["habs"][:, 0:1])
                ctx.inc(inst, "vm", 1)
        ctx.mark("C1.v.xmax", "vm")
        ctx.wait(v, "C1.ar1row")
        inst = v.reduce_max(rrow[0:1, 128:129], rrow[0:1, 0:128], axis=AX.X)
        ctx.inc(inst, "vm", 1)
        ctx.selfwait(v, "vm")
        ctx.mark("C1.v.gmaxred", "vm")
        ctx.wait(v, "C1.gmaxbc")
        ctx.wait(v, "C1.s.sig")
        inst = v.tensor_mul(scal[:, 2:4], scal[:, 0:2], scal[:, 12:14])
        ctx.inc(inst, "vm", 1)
        ctx.selfwait(v, "vm")
        inst = v.tensor_scalar_mul(scal[:, 2:4], scal[:, 2:4], 1.0 / QMAX)
        ctx.inc(inst, "vm", 1)
        ctx.selfwait(v, "vm")
        inst = v.reciprocal(scal[:, 4:6], scal[:, 2:4])
        ctx.inc(inst, "vm", 1)
        ctx.selfwait(v, "vm")
        ctx.mark("C1.v.invs", "vm")
        for ic in range(IC):
            inst = v.tensor_mul(sgi[:, ic:ic + 1], scal[:, 2:3], sg_all[:, ic:ic + 1])
            ctx.inc(inst, "vm", 1)
            inst = v.tensor_mul(sui[:, ic:ic + 1], scal[:, 2:3], su_all[:, ic:ic + 1])
            ctx.inc(inst, "vm", 1)
        ctx.selfwait(v, "vm")
        ctx.mark("C1.v.sgi", "vm")
        for gs in range(4):
            ctx.wait(v, f"C1.s.xq{gs}")
            for dc in range(8):
                tv = C["t1sq"][:, dc * 512:dc * 512 + 512]
                inst = v.tensor_scalar(tv, tv, MAGIC, QMAX, ALU.subtract, ALU.min)
                ctx.inc(inst, "vm", 1)
                ctx.selfwait(v, "vm")
                inst = v.tensor_scalar_max(C["xq"][:, dc * 2048 + 512 * gs:dc * 2048 + 512 * gs + 512],
                                           tv, -QMAX - 1.0)
                ctx.inc(inst, "vm", 1)
            ctx.mark(f"C1.v.xq{gs}", "vm")
        for ic in range(IC):
            for gs in range(4):
                k = ic * 4 + gs
                ctx.wait(v, f"C1.s.si{k}")
                if k >= 2:
                    ctx.wait(v, f"C1.hst{k - 2}")
                uslot = [C["ps_u"][:, 0:512], C["ps_M"][:],
                         C["ps_u"][:, 512:1024], C["ps_SQ"][:]][k % 4]
                inst = v.scalar_tensor_tensor(C["ht"][:, (k % 2) * 512:(k % 2) * 512 + 512],
                                              C["hsil"][:, (k % 2) * 512:(k % 2) * 512 + 512],
                                              sui[:, ic:ic + 1],
                                              uslot,
                                              ALU.mult, ALU.mult)
                ctx.inc(inst, "vm", 1)
                ctx.selfwait(v, "vm")
                ctx.mark(f"C1.v.h{k}", "vm")
                inst = v.reduce_max(C["habs"][:, 1:2], C["ht"][:, (k % 2) * 512:(k % 2) * 512 + 512],
                                    axis=AX.X, apply_absolute_value=True)
                ctx.inc(inst, "vm", 1)
                ctx.selfwait(v, "vm")
                inst = v.tensor_max(rmax[:, 1:2], rmax[:, 1:2], C["habs"][:, 1:2])
                ctx.inc(inst, "vm", 1)
                ctx.selfwait(v, "vm")
        ctx.mark("C1.v.hmax", "vm")
        ctx.barrier_inc(v)

    stC2 = ExitStack()

    def alloc_C2():
        def sc2(name, shape, dt):
            return stC2.enter_context(nc.sbuf_tensor(name, shape, dt))
        o = {}
        o["hq_all"] = sc2("hq_all", [128, IC * 2048], BF16)
        o["hstage"] = sc2("hstage", [128, 2 * 2048], F32)
        o["hfma"] = sc2("hfma", [128, 2048], F32)
        o["xln2"] = sc2("xln2", [128, 8 * 512], F32)
        o["wf2"] = sc2("wf2", [128, 4 * 1024], BF16)
        o["t2a"] = sc2("t2a", [128, 2 * 512], F32)
        o["t2"] = sc2("t2", [128, 8 * 512], F32)
        o["t2sq"] = sc2("t2sq", [128, 8 * 512], F32)
        o["mbc2"] = sc2("mbc2", [128, 512], F32)
        o["m2bc2"] = sc2("m2bc2", [128, 512], F32)
        o["varb2"] = sc2("varb2", [128, 512], F32)
        o["sdev2"] = sc2("sdev2", [128, 512], F32)
        o["invb2"] = sc2("invb2", [128, 512], F32)
        o["ob"] = sc2("ob", [128, 8 * 512], F32)
        o["ps_d"] = stC2.enter_context(nc.psum_tensor("ps_d", [128, 4096], F32))
        return o

    def C2_gpsimd(g, C):
        ctx.barrier_wait(g, 15)
        ctx.wait(g, "C1.ar2in")
        inst = g.collective_compute("AllReduce", ALU.max,
                                    replica_groups=[list(range(NCORE))],
                                    ins=[ar2_in.ap()], outs=[ar2_out.ap()])
        ctx.inc(inst, "gm", 1)
        ctx.selfwait(g, "gm")
        ctx.gdma(g, out=rrow[0:1, 0:128], in_=bass.AP(ar2_out, 0, [[1, 1], [1, 128]]))
        ctx.mark("C2.ar2row", "lg")
        ctx.wait(g, "C2.hmaxb")
        ctx.gdma(g, out=scal[:, 6:8], in_=bass.AP(sc_d, 4, [[0, 128], [0, 2], [1, 1]]))
        ctx.mark("C2.hmaxbc", "lg")
        ctx.selfwait(g, "lg")
        ctx.barrier_inc(g)

    def C2_sync(sy, C):
        ctx.barrier_wait(sy, 15)
        ctx.wait(sy, "C1.alldma")
        ctx.wait(sy, "C2.v.hmaxred")
        ctx.sdma(sy, key="C2.hmaxb", out=bass.AP(sc_d, 4, [[1, 1], [1, 1]]),
                 in_=rrow[0:1, 128:129])
        def dwload(k):
            if k >= 4:
                ctx.wait(sy, f"C2.t.dn{k - 4}")
            ctx.sdma(sy, key=f"C2.dw{k}",
                     out=C["wf2"][:].rearrange("p (b hc j) -> p b hc j", b=4, hc=8)[:, k % 4],
                     in_=bass.AP(wq_d["d"], (k % IC) * 8 * 128 * 128,
                                 [[128, 128], [128 * 128, 8], [1, 128]]))

        for k in range(4):
            dwload(k)
        for ic in range(IC):
            if ic >= 2:
                ctx.wait(sy, f"C2.s.hfma{ic - 2}")
            ctx.sdma(sy, key=f"C2.hld{ic}",
                     out=C["hstage"][:, (ic % 2) * 2048:(ic % 2) * 2048 + 2048],
                     in_=bass.AP(h_d, ic * 128 * TPC, [[TPC, 128], [1, 2048]]))
        for gs in range(4):
            if gs > 0:
                ctx.wait(sy, f"C2.v.t2all{gs - 1}")
            ctx.sdma(sy, key=f"C2.xln{gs}",
                     out=C["xln2"][:].rearrange("p (dc j) -> p dc j", dc=8),
                     in_=bass.AP(xln_d, 512 * gs, [[TPC, 128], [128 * TPC, 8], [1, 512]]))
            for ic in range(IC):
                k = gs * IC + ic
                if k < 4:
                    continue
                dwload(k)
            ctx.wait(sy, f"C2.v.out{gs}")
            ctx.sdma(sy, key=f"C2.out{gs}",
                     out=bass.AP(outT, 512 * gs, [[TPC, 128], [128 * TPC, 8], [1, 512]]),
                     in_=C["ob"][:].rearrange("p (dc j) -> p dc j", dc=8))
        ctx.sdrain(sy)

    def C2_tensor(t, C):
        ctx.barrier_wait(t, 15)
        for gs in range(4):
            for ic in range(IC):
                k = gs * IC + ic
                ctx.wait(t, f"C2.dw{k}")
                ctx.wait(t, f"C2.v.hq{ic}")
                if ic == 0 and gs > 0:
                    ctx.wait(t, f"C2.s.dscall{gs - 1}")
                    ctx.wait(t, f"C2.v.var2_{gs - 1}")
                inst = None
                for hc in range(8):
                    inst = t.matmul(C["ps_d"][:, hc * 512:hc * 512 + 512],
                                    C["wf2"][:, (k % 4) * 1024 + hc * 128:(k % 4) * 1024 + hc * 128 + 128],
                                    C["hq_all"][:, ic * 2048 + 512 * gs:ic * 2048 + 512 * gs + 512],
                                    start=(ic == 0), stop=(ic == IC - 1))
                ctx.inc(inst, "tm", 1)
                ctx.mark(f"C2.t.dn{k}", "tm")
            ctx.wait(t, f"C2.v.t2all{gs}")
            ctx.wait(t, f"C2.v.sq2all{gs}")
            inst = None
            for dc in range(8):
                inst = t.matmul(C["ps_d"][:, 0:512], ones128[:], C["t2"][:, dc * 512:dc * 512 + 512],
                                start=(dc == 0), stop=(dc == 7))
            for dc in range(8):
                inst = t.matmul(C["ps_d"][:, 512:1024], ones128[:], C["t2sq"][:, dc * 512:dc * 512 + 512],
                                start=(dc == 0), stop=(dc == 7))
            ctx.inc(inst, "tm", 1)
            ctx.mark(f"C2.t.st{gs}", "tm")
        ctx.selfwait(t, "tm")
        ctx.barrier_inc(t)

    def C2_scalar(s_, C):
        ctx.barrier_wait(s_, 15)
        for ic in range(IC):
            ctx.wait(s_, f"C2.hld{ic}")
            ctx.wait(s_, "C2.v.invs2")
            if ic >= 1:
                ctx.wait(s_, f"C2.v.hq{ic - 1}")
            inst = s_.activation(C["hfma"][:], C["hstage"][:, (ic % 2) * 2048:(ic % 2) * 2048 + 2048],
                                 AF.Identity, bias=magic_sb[:, 0:1], scale=scal[:, 10:11])
            ctx.inc(inst, "sm", 1)
            ctx.mark(f"C2.s.hfma{ic}", "sm")
        for gs in range(4):
            for dc in range(8):
                k = gs * 8 + dc
                ctx.wait(s_, f"C2.t.dn{gs * IC + IC - 1}")
                ctx.wait(s_, "C2.v.sdi")
                if k >= 2:
                    ctx.wait(s_, f"C2.v.t2_{k - 2}")
                inst = s_.activation(C["t2a"][:, (k % 2) * 512:(k % 2) * 512 + 512],
                                     C["ps_d"][:, dc * 512:dc * 512 + 512],
                                     AF.Identity, scale=sdi[:, dc:dc + 1])
                ctx.inc(inst, "sm", 1)
                ctx.mark(f"C2.s.dsc{k}", "sm")
            ctx.mark(f"C2.s.dscall{gs}", "sm")
            ctx.wait(s_, f"C2.v.var2_{gs}")
            inst = s_.activation(C["sdev2"][:], C["varb2"][:], AF.Sqrt)
            ctx.inc(inst, "sm", 1)
            ctx.mark(f"C2.s.sqrt2{gs}", "sm")
        ctx.selfwait(s_, "sm")
        ctx.barrier_inc(s_)

    def C2_vector(v, C):
        ctx.barrier_wait(v, 15)
        ctx.wait(v, "C2.ar2row")
        inst = v.reduce_max(rrow[0:1, 128:129], rrow[0:1, 0:128], axis=AX.X)
        ctx.inc(inst, "vm", 1)
        ctx.selfwait(v, "vm")
        ctx.mark("C2.v.hmaxred", "vm")
        ctx.wait(v, "C2.hmaxbc")
        inst = v.tensor_mul(scal[:, 8:10], scal[:, 6:8], scal[:, 14:16])
        ctx.inc(inst, "vm", 1)
        ctx.selfwait(v, "vm")
        inst = v.tensor_scalar_mul(scal[:, 8:10], scal[:, 8:10], 1.0 / QMAX)
        ctx.inc(inst, "vm", 1)
        ctx.selfwait(v, "vm")
        ctx.mark("C2.v.s2", "vm")
        inst = v.reciprocal(scal[:, 10:12], scal[:, 8:10])
        ctx.inc(inst, "vm", 1)
        ctx.selfwait(v, "vm")
        ctx.mark("C2.v.invs2", "vm")
        for dc in range(8):
            inst = v.tensor_mul(sdi[:, dc:dc + 1], scal[:, 8:9], sd_all[:, dc:dc + 1])
            ctx.inc(inst, "vm", 1)
        ctx.selfwait(v, "vm")
        ctx.mark("C2.v.sdi", "vm")
        for ic in range(IC):
            ctx.wait(v, f"C2.s.hfma{ic}")
            inst = v.tensor_scalar(C["hfma"][:], C["hfma"][:], MAGIC, QMAX, ALU.subtract, ALU.min)
            ctx.inc(inst, "vm", 1)
            ctx.selfwait(v, "vm")
            inst = v.tensor_scalar_max(C["hq_all"][:, ic * 2048:ic * 2048 + 2048], C["hfma"][:],
                                       -QMAX - 1.0)
            ctx.inc(inst, "vm", 1)
            ctx.mark(f"C2.v.hq{ic}", "vm")
        for gs in range(4):
            ctx.wait(v, f"C2.xln{gs}")
            for dc in range(8):
                k = gs * 8 + dc
                ctx.wait(v, f"C2.s.dsc{k}")
                inst = v.tensor_add(C["t2"][:, dc * 512:dc * 512 + 512],
                                    C["t2a"][:, (k % 2) * 512:(k % 2) * 512 + 512],
                                    C["xln2"][:, dc * 512:dc * 512 + 512])
                ctx.inc(inst, "vm", 1)
                ctx.mark(f"C2.v.t2_{k}", "vm")
                ctx.selfwait(v, "vm")
                inst = v.tensor_mul(C["t2sq"][:, dc * 512:dc * 512 + 512],
                                    C["t2"][:, dc * 512:dc * 512 + 512],
                                    C["t2"][:, dc * 512:dc * 512 + 512])
                ctx.inc(inst, "vm", 1)
            ctx.mark(f"C2.v.t2all{gs}", "vm")
            ctx.mark(f"C2.v.sq2all{gs}", "vm")
            ctx.wait(v, f"C2.t.st{gs}")
            inst = v.tensor_scalar_mul(C["mbc2"][:], C["ps_d"][:, 0:512], 1.0 / D)
            ctx.inc(inst, "vm", 1)
            ctx.selfwait(v, "vm")
            inst = v.tensor_mul(C["m2bc2"][:], C["mbc2"][:], C["mbc2"][:])
            ctx.inc(inst, "vm", 1)
            ctx.selfwait(v, "vm")
            inst = v.tensor_scalar_sub(C["m2bc2"][:], C["m2bc2"][:], 1e-5)
            ctx.inc(inst, "vm", 1)
            ctx.selfwait(v, "vm")
            inst = v.scalar_tensor_tensor(C["varb2"][:], C["ps_d"][:, 512:1024], 1.0 / D, C["m2bc2"][:],
                                          ALU.mult, ALU.subtract)
            ctx.inc(inst, "vm", 1)
            ctx.mark(f"C2.v.var2_{gs}", "vm")
            ctx.wait(v, f"C2.s.sqrt2{gs}")
            for c0 in range(0, 512, 256):
                inst = v.reciprocal(C["invb2"][:, c0:c0 + 256], C["sdev2"][:, c0:c0 + 256])
                ctx.inc(inst, "vm", 1)
            ctx.selfwait(v, "vm")
            if gs > 0:
                ctx.wait(v, f"C2.out{gs - 1}")
            for dc in range(8):
                inst = v.scalar_tensor_tensor(C["ob"][:, dc * 512:dc * 512 + 512],
                                              C["mbc2"][:], -1.0, C["t2"][:, dc * 512:dc * 512 + 512],
                                              ALU.mult, ALU.add)
                ctx.inc(inst, "vm", 1)
            ctx.selfwait(v, "vm")
            for dc in range(8):
                obv = C["ob"][:, dc * 512:dc * 512 + 512]
                inst = v.tensor_mul(obv, obv, C["invb2"][:])
                ctx.inc(inst, "vm", 1)
            ctx.mark(f"C2.v.out{gs}", "vm")
        ctx.barrier_inc(v)

    def DBG_A_sync(sy, _):
        ctx.barrier_wait(sy, 5)
        for key, srcap in (("kh", k_h[:, 0:NKS * KS]), ("va", v_aug[:]), ("qh", q_h[:])):
            ctx.sdma(sy, out=dbg[key].ap(), in_=srcap)
        ctx.barrier_wait(sy, 10)
        ctx.sdma(sy, out=dbg["wq"].ap(), in_=wq_d["g"].ap())
        ctx.sdma(sy, out=dbg["wd"].ap(), in_=wq_d["d"].ap())
        ctx.sdrain(sy)

    def DBG_B_sync(sy, _):
        ctx.barrier_wait(sy, 10)
        ctx.sdma(sy, out=dbg["oin"].ap(), in_=o_in.ap())
        ctx.sdrain(sy)

    def run_phase(alloc_fn, emitters, close_stacks=()):
        objs = alloc_fn() if alloc_fn else None
        import os
        if os.environ.get("KDBG"):
            print("PHASE", emitters[0][1].__name__, "sbuf remaining:", nc.sbuf_bytes_remaining)
        snap = dict(ctx.cnt)
        ctx.mock = True
        for _, fn in emitters:
            fn(_MOCK, objs)
        ctx.cnt = snap
        ctx.mock = False
        with nc.Block() as blk:
            for name, fn in emitters:
                getattr(blk, name)(lambda eng, _fn=fn: _fn(eng, objs))
        for st_ in close_stacks:
            st_.close()

    phA = [("gpsimd", A_gpsimd), ("sync", A_sync), ("tensor", A_tensor),
           ("vector", A_vector), ("scalar", A_scalar)]
    phB = [("tensor", B_tensor), ("scalar", B_scalar), ("vector", B_vector),
           ("sync", B_sync), ("gpsimd", B_gpsimd)]
    phC1 = [("gpsimd", C1_gpsimd), ("sync", C1_sync), ("tensor", C1_tensor),
            ("scalar", C1_scalar), ("vector", C1_vector)]
    phC2 = [("gpsimd", C2_gpsimd), ("sync", C2_sync), ("tensor", C2_tensor),
            ("scalar", C2_scalar), ("vector", C2_vector)]

    if stage == "A":
        run_phase(alloc_A, phA, (stA,))
        run_phase(alloc_B, phB, (stB,))
        run_phase(None, [("sync", DBG_A_sync)])
    elif stage == "B":
        run_phase(alloc_A, phA, (stA,))
        run_phase(alloc_B, phB, (stB,))
        run_phase(None, [("sync", DBG_B_sync)])
    else:
        run_phase(alloc_A, phA, (stA,))
        run_phase(alloc_B, phB, (stB, stAB))
        run_phase(alloc_C1, phC1, (stC1,))
        run_phase(alloc_C2, phC2, ())
    return nc


_NC_CACHE = {}

def _get_nc(stage=None):
    if stage not in _NC_CACHE:
        _NC_CACHE[stage] = build_nc(stage)
    return _NC_CACHE[stage]


def make_in_maps(inputs):
    x = np.ascontiguousarray(np.asarray(inputs["L0_attn_x"], dtype=np.float32)[0].T)
    wq = np.asarray(inputs["L0_attn_Wq"], dtype=np.float32)
    wk = np.asarray(inputs["L0_attn_Wk"], dtype=np.float32)
    wv = np.asarray(inputs["L0_attn_Wv"], dtype=np.float32)
    wo = np.asarray(inputs["L0_attn_Wo"], dtype=np.float32)
    bo_ = np.asarray(inputs["L0_attn_bo"], dtype=np.float32)
    wcv = np.asarray(inputs["L0_attn_Wconv"], dtype=np.float32)
    wconvN_ = np.ascontiguousarray(wcv.transpose(0, 2, 1).reshape(D, 3072))
    wqT = np.ascontiguousarray(wq.T)
    wkT = np.ascontiguousarray(wk.T)
    wvT = np.ascontiguousarray(wv.T)
    woT_ = np.ascontiguousarray(wo.T)
    wg = np.asarray(inputs["L0_ffn_w_gate"], dtype=np.float32)
    wu = np.asarray(inputs["L0_ffn_w_up"], dtype=np.float32)
    wd = np.asarray(inputs["L0_ffn_w_down"], dtype=np.float32)
    maps = []
    for c in range(NCORE):
        maps.append({
            "xT": x,
            "xres": np.ascontiguousarray(x[:, TPC * c:TPC * (c + 1)]),
            "cid": np.array([[c]], dtype=np.int32),
            "coff": np.array([[c * (128 * NCORE) * (T // 8) + c2 * 2 * 128 * (T // 8)
                               for c2 in range(4)]], dtype=np.int32),
            "wkTc": np.ascontiguousarray(wkT[:, 128 * c:128 * c + 128]),
            "wvTc": np.ascontiguousarray(wvT[:, 128 * c:128 * c + 128]),
            "wqTc": np.ascontiguousarray(wqT[:, 128 * c:128 * c + 128]),
            "wconvN": wconvN_,
            "woT": woT_,
            "bo": bo_,
            "wgN": wg, "wgT": np.ascontiguousarray(wg.T),
            "wuN": wu, "wuT": np.ascontiguousarray(wu.T),
            "wdN": wd, "wdT": np.ascontiguousarray(wd.T),
            "clipx": np.asarray(inputs["L0_ffn_clip_x"], dtype=np.float32),
            "clipd": np.asarray(inputs["L0_ffn_clip_down"], dtype=np.float32),
        })
    return maps


def kernel(**inputs):
    nc = _get_nc(None)
    maps = make_in_maps(inputs)
    res = run_bass_kernel_spmd(nc, maps, core_ids=list(range(NCORE)))
    out = np.empty((1, T, D), dtype=np.float32)
    for c in range(NCORE):
        out[0, TPC * c:TPC * (c + 1), :] = res.results[c]["outT"].T
    return out


# version 21
# speedup vs baseline: 1.0185x; 1.0185x over previous
import numpy as np
import concourse.bass as bass
import concourse.mybir as mybir
from concourse.bass_utils import run_bass_kernel_spmd
from contextlib import ExitStack

F32 = mybir.dt.float32
BF16 = mybir.dt.bfloat16
I32 = mybir.dt.int32
AF = mybir.ActivationFunctionType
ALU = mybir.AluOpType
AX = mybir.AxisListType

NCORE = 8
D = 1024
T = 16384
DI = 2816
TQ = 1024
NT = 16
KS = 128
NKS = 43
NSEG = 43
SEGW = 389
IC = DI // 128
TPC = T // NCORE
MAGIC = 3.0 * 2.0 ** 22
QMAX = 127.0

SUBS = [[s for s in range(NKS) if 3 * KS * s <= TQ * t + TQ - 1] for t in range(NT)]

def qx_of(t, s):
    return 3 * KS * s - TQ * t

def ql_of(t, s):
    q = qx_of(t, s)
    return 0 if q <= 0 else min(TQ, (q // 512) * 512)

def qhb_of(t, s):
    return min(TQ, 3 * (KS * s + 127) + 1 - TQ * t)

def pv_c0(t, s):
    return max(0, qx_of(t, s)) // 512

def last_sub(t, c):
    return max(s for s in SUBS[t] if pv_c0(t, s) <= c)

def off(c, h):
    return (2 * c + h) * 512

def mask_pieces(t, s):
    b0 = max(0, qx_of(t, s))
    b1 = min(TQ, qhb_of(t, s))
    if b1 <= b0:
        return []
    out = []
    for h in range(2):
        a = b0
        while a < b1:
            b = min(b1, (a // 512 + 1) * 512)
            out.append((h, a, b))
            a = b
    return out

TORDER = [12, 13, 14, 15, 8, 9, 10, 11, 4, 5, 6, 7, 0, 1, 2, 3]
PREV_T = {TORDER[i]: TORDER[i - 1] for i in range(1, 16)}
TPOS = {t: i for i, t in enumerate(TORDER)}
GID = {}
_g = 0
for _t in TORDER:
    for _i in range(len(SUBS[_t])):
        _g += 1
        GID[(_t, _i)] = _g
NG = _g

QIT = [(w, ic) for ic in range(IC) for w in ("g", "u")] + [("d", ic) for ic in range(IC)]
QSC = [("g", ic) for ic in range(IC)] + [("u", ic) for ic in range(IC)] + [("d", hc) for hc in range(8)]
QSEG = [min(NSEG - 1, qi * NSEG // 52) for qi in range(52)]
QB = [[] for _ in range(NT)]
for _qi in range(len(QIT)):
    QB[min(NT - 1, _qi * NT // len(QIT))].append(_qi)

def seg_cols(s):
    tk0 = 384 * s - 3
    a = max(0, tk0)
    b = min(T, tk0 + SEGW)
    return a, b - a, a - tk0

def seg_n_q(s):
    return min(384, T - 384 * s)


class _FakeInst:
    def then_inc(self, sem, k=1):
        return self

_FAKE = _FakeInst()

class _MockEng:
    def __getattr__(self, name):
        if name == "value_load":
            return lambda *a, **k: 0
        return lambda *a, **k: _FAKE

_MOCK = _MockEng()


class Ctx:
    def __init__(self):
        self.mock = True
        self.cnt = {}
        self.marks = {}
        self.sems = {}

    def inc(self, inst, sem, k=1):
        self.cnt[sem] = self.cnt.get(sem, 0) + k
        if not self.mock:
            inst.then_inc(self.sems[sem], k)
        return inst

    def mark(self, key, sem):
        v = self.cnt.get(sem, 0)
        if self.mock:
            assert key not in self.marks, key
            self.marks[key] = (sem, v)
        else:
            assert self.marks[key] == (sem, v), (key, self.marks[key], (sem, v))

    def wait(self, eng, key):
        if not self.mock:
            sem, v = self.marks[key]
            if v > 0:
                eng.wait_ge(self.sems[sem], v)

    def selfwait(self, eng, sem):
        if not self.mock and self.cnt.get(sem, 0) > 0:
            eng.wait_ge(self.sems[sem], self.cnt[sem])

    def gdma(self, g, **kw):
        self.selfwait(g, "lg")
        inst = g.dma_start(**kw)
        return self.inc(inst, "lg", 16)

    def sdma(self, sy, key=None, **kw):
        i = self.cnt.get("_sdseq", 0)
        self.cnt["_sdseq"] = i + 1
        sem = f"ld{i % 8}"
        self.selfwait(sy, sem)
        inst = sy.dma_start(**kw)
        self.inc(inst, sem, 16)
        if key is not None:
            self.mark(key, sem)
        return inst

    def sdrain(self, sy, key=None):
        for j in range(8):
            self.selfwait(sy, f"ld{j}")
        if key is not None:
            self.mark(key, "ld0")

    def barrier_inc(self, eng):
        self.cnt["ph"] = self.cnt.get("ph", 0) + 1
        if not self.mock:
            eng.sem_inc(self.sems["ph"], 1)

    def barrier_wait(self, eng, v):
        if not self.mock:
            eng.wait_ge(self.sems["ph"], v)


def build_nc(stage=None):
    nc = bass.Bass("TRN2", target_bir_lowering=False)

    xT = nc.dram_tensor("xT", [D, T], F32, kind="ExternalInput")
    xres = nc.dram_tensor("xres", [D, TPC], F32, kind="ExternalInput")
    cid = nc.dram_tensor("cid", [1, 1], I32, kind="ExternalInput")
    coff = nc.dram_tensor("coff", [1, 4], I32, kind="ExternalInput")
    wkTc = nc.dram_tensor("wkTc", [D, 128], F32, kind="ExternalInput")
    wvTc = nc.dram_tensor("wvTc", [D, 128], F32, kind="ExternalInput")
    wqTc = nc.dram_tensor("wqTc", [D, 128], F32, kind="ExternalInput")
    wconvN = nc.dram_tensor("wconvN", [D, 3072], F32, kind="ExternalInput")
    woT = nc.dram_tensor("woT", [D, D], F32, kind="ExternalInput")
    bo = nc.dram_tensor("bo", [D], F32, kind="ExternalInput")
    wgN = nc.dram_tensor("wgN", [DI, D], F32, kind="ExternalInput")
    wgT = nc.dram_tensor("wgT", [D, DI], F32, kind="ExternalInput")
    wuN = nc.dram_tensor("wuN", [DI, D], F32, kind="ExternalInput")
    wuT = nc.dram_tensor("wuT", [D, DI], F32, kind="ExternalInput")
    wdN = nc.dram_tensor("wdN", [D, DI], F32, kind="ExternalInput")
    wdT = nc.dram_tensor("wdT", [DI, D], F32, kind="ExternalInput")
    clipx = nc.dram_tensor("clipx", [1], F32, kind="ExternalInput")
    clipd = nc.dram_tensor("clipd", [1], F32, kind="ExternalInput")

    dbg = {}
    if stage is None:
        outT = nc.dram_tensor("outT", [D, TPC], F32, kind="ExternalOutput")
    elif stage == "A":
        dbg["kh"] = nc.dram_tensor("dbg_kh", [128, NKS * KS], BF16, kind="ExternalOutput")
        dbg["va"] = nc.dram_tensor("dbg_va", [128, NKS * 256], BF16, kind="ExternalOutput")
        dbg["qh"] = nc.dram_tensor("dbg_qh", [128, T], BF16, kind="ExternalOutput")
        dbg["wq"] = nc.dram_tensor("dbg_wq", [IC * 8 * 128 * 128], BF16, kind="ExternalOutput")
        dbg["wd"] = nc.dram_tensor("dbg_wd", [IC * 8 * 128 * 128], BF16, kind="ExternalOutput")
    elif stage == "B":
        dbg["oin"] = nc.dram_tensor("dbg_oin", [128, T], BF16, kind="ExternalOutput")

    o_in = nc.dram_tensor("o_in", [8, 128, T // 8], BF16)
    o_ag = nc.dram_tensor("o_ag", [8, NCORE * 128, T // 8], BF16)
    ar1_in = nc.dram_tensor("ar1_in", [128], F32)
    ar1_out = nc.dram_tensor("ar1_out", [128], F32)
    ar2_in = nc.dram_tensor("ar2_in", [128], F32)
    ar2_out = nc.dram_tensor("ar2_out", [128], F32)
    xln_d = nc.dram_tensor("xln_d", [D, TPC], F32)
    h_d = nc.dram_tensor("h_d", [DI, TPC], F32)
    wq_d = {"g": nc.dram_tensor("wgq_d", [IC * 8 * 128 * 128], BF16),
            "u": nc.dram_tensor("wuq_d", [IC * 8 * 128 * 128], BF16),
            "d": nc.dram_tensor("wdq_d", [IC * 8 * 128 * 128], BF16)}
    rs_d = nc.dram_tensor("rs_d", [3 * DI], F32)
    v0_d = nc.dram_tensor("v0_d", [128], F32)
    sc_d = nc.dram_tensor("sc_d", [16], F32)

    ctx = Ctx()
    perm = ExitStack()
    def psb(name, shape, dt):
        return perm.enter_context(nc.sbuf_tensor(name, shape, dt))
    sg_all = psb("sg_all", [128, IC], F32)
    su_all = psb("su_all", [128, IC], F32)
    sd_all = psb("sd_all", [128, 8], F32)
    ones128 = psb("ones128", [128, 128], F32)
    cid_sb = psb("cid_sb", [1, 1], I32)
    coff_sb = psb("coff_sb", [1, 4], I32)
    scal = psb("scal", [128, 16], F32)
    rmax = psb("rmax", [128, 2], F32)
    magic_sb = psb("magic_sb", [128, 1], F32)
    rrow = psb("rrow", [1, 130], F32)
    sgi = psb("sgi", [128, IC], F32)
    sui = psb("sui", [128, IC], F32)
    sdi = psb("sdi", [128, 8], F32)
    for s in ("lg", "tm", "vm", "sm", "gm", "ph", "sy",
              "ld0", "ld1", "ld2", "ld3", "ld4", "ld5", "ld6", "ld7"):
        ctx.sems[s] = perm.enter_context(nc.semaphore(s))

    stAB = ExitStack()
    def ab(name, shape, dt):
        return stAB.enter_context(nc.sbuf_tensor(name, shape, dt))
    q_h = ab("q_h", [128, T], BF16)
    k_h = ab("k_h", [128, NKS * KS + 128], BF16)
    v_aug = ab("v_aug", [128, NKS * 256], BF16)
    m384 = ab("m384", [128, 384], BF16)
    wtile = ab("wtile", [128, 4 * 1024], F32)
    rs_bc = ab("rs_bc", [128, 4 * 1024], F32)
    wqb = ab("wqb", [128, 6 * 1024], BF16)
    rsg_all = ab("rsg_all", [128, IC], F32)
    rsu_all = ab("rsu_all", [128, IC], F32)
    rsd_all = ab("rsd_all", [128, 8], F32)

    v_aug4 = lambda: v_aug[:].rearrange("p (s h c) -> p s h c", s=NKS, h=2)

    stA = ExitStack()

    def alloc_A():
        def sa(name, shape, dt):
            return stA.enter_context(nc.sbuf_tensor(name, shape, dt))
        o = {}
        o["wcv"] = sa("wcv", [128, 8 * 3072], BF16)
        o["wkc"] = sa("wkc", [128, 8 * 128], BF16)
        o["wvc"] = sa("wvc", [128, 8 * 128], BF16)
        o["wqc"] = sa("wqc", [128, 8 * 128], BF16)
        o["wkcT"] = sa("wkcT", [128, 24 * 128], BF16)
        o["wvcT"] = sa("wvcT", [128, 24 * 128], BF16)
        o["xw"] = sa("xw", [128, 2 * 8 * SEGW], BF16)
        o["xw3"] = sa("xw3", [128, 2 * 3 * 8 * 130], BF16)
        o["x0b"] = sa("x0b", [128, 8], BF16)
        o["v0c"] = sa("v0c", [128, 2], F32)
        o["mones"] = sa("mones", [128, 384], BF16)
        o["wnat"] = sa("wnat", [128, 1408], F32)
        o["absw"] = sa("absw", [128, 2], F32)
        o["ps_w"] = stA.enter_context(nc.psum_tensor("ps_w", [128, 128], F32))
        o["ps_k"] = stA.enter_context(nc.psum_tensor("ps_k", [128, 1024], F32))
        o["ps_v"] = stA.enter_context(nc.psum_tensor("ps_v", [128, 1024], F32))
        o["ps_q"] = stA.enter_context(nc.psum_tensor("ps_q", [128, 1024], F32))
        return o

    def A_gpsimd(g, A):
        ctx.gdma(g, out=A["wcv"][:].rearrange("p (c m) -> p c m", c=8),
                 in_=bass.AP(wconvN, 0, [[3072, 128], [128 * 3072, 8], [1, 3072]]))
        for nsb, wh in (("wkc", wkTc), ("wvc", wvTc), ("wqc", wqTc)):
            ctx.gdma(g, out=A[nsb][:].rearrange("p (c m) -> p c m", c=8),
                     in_=bass.AP(wh, 0, [[128, 128], [128 * 128, 8], [1, 128]]))
        ctx.gdma(g, out=A["x0b"][:].rearrange("p (c u) -> p c u", u=1),
                 in_=bass.AP(xT, 0, [[T, 128], [128 * T, 8], [1, 1]]),
                 allow_slow_non_contiguous=True)
        ctx.gdma(g, out=cid_sb[:], in_=cid.ap())
        ctx.gdma(g, out=coff_sb[:], in_=coff.ap())
        ctx.mark("A.w", "lg")
        inst = g.memset(A["mones"][:], 1.0)
        ctx.inc(inst, "gm", 1)
        ctx.selfwait(g, "gm")
        inst = g.affine_select(m384[:], A["mones"][:], pattern=[[1, 384]],
                               compare_op=ALU.is_ge, fill=0.0, base=0,
                               channel_multiplier=-3)
        ctx.inc(inst, "gm", 1)
        inst = g.memset(v_aug4()[:, :, :, 64:128], 1.0)
        ctx.inc(inst, "gm", 1)
        ctx.mark("A.ginit", "gm")
        xw4 = A["xw"][:].rearrange("p (b c m) -> p b c m", b=2, c=8)
        for s in range(NSEG):
            a, n, doff = seg_cols(s)
            if s >= 2:
                ctx.wait(g, f"A.v.x3{s - 2}")
                ctx.wait(g, f"A.t.seg{s - 2}q")
            if doff > 0:
                g.memset(xw4[:, s % 2, :, 0:doff], 0.0)
            if doff + n < SEGW:
                g.memset(xw4[:, s % 2, :, doff + n:SEGW], 0.0)
            ctx.gdma(g, out=xw4[:, s % 2, :, doff:doff + n],
                     in_=bass.AP(xT, a, [[T, 128], [128 * T, 8], [1, n]]))
            ctx.mark(f"A.seg{s}", "lg")
        ctx.wait(g, "A.v0row")
        ctx.wait(g, "A.v.seg0v")
        for h in range(2):
            ctx.gdma(g, out=v_aug[0:1, h * 128:h * 128 + 64],
                     in_=bass.AP(v0_d, 64 * h, [[1, 1], [1, 64]]))
        ctx.mark("A.v0aug", "lg")
        ctx.selfwait(g, "lg")
        ctx.selfwait(g, "gm")
        ctx.barrier_inc(g)

    def A_sync(sy, A):
        for qi, (w, ic) in enumerate(QSC):
            if qi >= 1:
                ctx.wait(sy, f"A.v.qn{qi - 1}")
            if w == "d":
                ctx.sdma(sy, key=f"A.nat{qi}", out=A["wnat"][:, 0:1408],
                         in_=bass.AP(wdN, ic * 128 * DI, [[DI, 128], [1, 1408]]))
                ctx.wait(sy, f"A.v.qd{qi}")
                ctx.sdma(sy, key=f"A.nat{qi}b", out=A["wnat"][:, 0:1408],
                         in_=bass.AP(wdN, ic * 128 * DI + 1408, [[DI, 128], [1, 1408]]))
            else:
                src = bass.AP(wgN if w == "g" else wuN, ic * 128 * D, [[D, 128], [1, 1024]])
                ctx.sdma(sy, key=f"A.nat{qi}", out=A["wnat"][:, 0:1024], in_=src)
        for qi, (w, ic) in enumerate(QSC):
            ctx.wait(sy, f"A.v.qn{qi}")
            ral = {"g": rsg_all, "u": rsu_all, "d": rsd_all}[w]
            base = {"g": 0, "u": DI, "d": 2 * DI}[w]
            ctx.sdma(sy, out=bass.AP(rs_d, base + ic * 128, [[1, 128], [1, 1]]),
                     in_=ral[:, ic:ic + 1])
        ctx.wait(sy, "A.v.v0c")
        ctx.sdma(sy, key="A.v0row", out=bass.AP(v0_d, 0, [[1, 128], [1, 1]]), in_=A["v0c"][:, 0:1])
        ctx.sdrain(sy)
        self_inc = sy.sem_inc(ctx.sems["sy"], 1) if not ctx.mock else None
        ctx.cnt["sy"] = ctx.cnt.get("sy", 0) + 1
        ctx.mark("A.rsd", "sy")
        ctx.barrier_inc(sy)

    def A_tensor(t, A):
        ctx.wait(t, "A.w")
        ctx.wait(t, "A.v.ones")
        wcv4 = A["wcv"][:].rearrange("p (c m) -> p c m", c=8)
        widx = 0
        for tag, wsl, dst in (("k", A["wkc"], A["wkcT"]), ("v", A["wvc"], A["wvcT"])):
            for mt in range(24):
                if widx >= 1:
                    pk, pm = ("k", mt - 1) if tag == "k" else (("k", 23) if mt == 0 else ("v", mt - 1))
                    ctx.wait(t, f"A.v.w{pk}{pm}")
                inst = None
                for icb in range(8):
                    inst = t.matmul(A["ps_w"][:], wcv4[:, icb, mt * 128:mt * 128 + 128],
                                    wsl[:, icb * 128:icb * 128 + 128],
                                    start=(icb == 0), stop=(icb == 7))
                ctx.inc(inst, "tm", 1)
                ctx.mark(f"A.t.w{tag}{mt}", "tm")
                widx += 1
        ctx.wait(t, "A.v.wv23")
        inst = None
        for icb in range(8):
            inst = t.matmul(A["ps_w"][:, 0:1], A["wkc"][:, icb * 128:icb * 128 + 128],
                            A["x0b"][:, icb:icb + 1], start=(icb == 0), stop=(icb == 7))
        ctx.inc(inst, "tm", 1)
        ctx.mark("A.t.k0", "tm")
        ctx.wait(t, "A.v.k0e")
        for icb in range(8):
            inst = t.matmul(A["ps_w"][:, 0:1], A["wvc"][:, icb * 128:icb * 128 + 128],
                            A["x0b"][:, icb:icb + 1], start=(icb == 0), stop=(icb == 7))
        ctx.inc(inst, "tm", 1)
        ctx.mark("A.t.v0", "tm")
        for s in range(NSEG):
            ctx.wait(t, f"A.v.x3{s}")
            if s >= 2:
                ctx.wait(t, f"A.sc.k{s - 2}")
                ctx.wait(t, f"A.v.seg{s - 2}v")
                ctx.wait(t, f"A.sc.q{s - 2}")
            else:
                ctx.wait(t, "A.v.k0v0")
            x3b = (s % 2) * 3 * 8 * 130
            inst = None
            for mc in range(24):
                w, dinc = mc // 8, mc % 8
                a0 = x3b + (w * 8 + dinc) * 130
                inst = t.matmul(A["ps_k"][:, (s % 2) * 512:(s % 2) * 512 + 128],
                                A["wkcT"][:, mc * 128:mc * 128 + 128],
                                A["xw3"][:, a0 + 1:a0 + 129],
                                start=(mc == 0), stop=(mc == 23))
            ctx.inc(inst, "tm", 1)
            ctx.mark(f"A.t.seg{s}k", "tm")
            for mc in range(24):
                w, dinc = mc // 8, mc % 8
                a0 = x3b + (w * 8 + dinc) * 130
                inst = t.matmul(A["ps_v"][:, (s % 2) * 512:(s % 2) * 512 + 128],
                                A["xw3"][:, a0:a0 + 128],
                                A["wvcT"][:, mc * 128:mc * 128 + 128],
                                start=(mc == 0), stop=(mc == 23))
            ctx.inc(inst, "tm", 1)
            ctx.mark(f"A.t.seg{s}v", "tm")
            nq = seg_n_q(s)
            ctx.wait(t, f"A.seg{s}")
            for icb in range(8):
                a0 = (s % 2) * 8 * SEGW + icb * SEGW + 3
                inst = t.matmul(A["ps_q"][:, (s % 2) * 512:(s % 2) * 512 + nq],
                                A["wqc"][:, icb * 128:icb * 128 + 128],
                                A["xw"][:, a0:a0 + nq],
                                start=(icb == 0), stop=(icb == 7))
            ctx.inc(inst, "tm", 1)
            ctx.mark(f"A.t.seg{s}q", "tm")
            ctx.mark(f"A.t.seg{s}", "tm")
        ctx.selfwait(t, "tm")
        ctx.barrier_inc(t)

    def A_vector(v, A):
        v.memset(ones128[:], 1.0)
        v.memset(magic_sb[:], MAGIC)
        inst = v.memset(rmax[:], 0.0)
        ctx.inc(inst, "vm", 1)
        ctx.mark("A.v.ones", "vm")
        for tag, dst in (("k", A["wkcT"]), ("v", A["wvcT"])):
            for mt in range(24):
                ctx.wait(v, f"A.t.w{tag}{mt}")
                inst = v.tensor_copy(dst[:, mt * 128:mt * 128 + 128], A["ps_w"][:])
                ctx.inc(inst, "vm", 1)
                ctx.mark(f"A.v.w{tag}{mt}", "vm")
        ctx.wait(v, "A.t.k0")
        inst = v.tensor_copy(k_h[:, 0:1], A["ps_w"][:, 0:1])
        ctx.inc(inst, "vm", 1)
        ctx.mark("A.v.k0e", "vm")
        ctx.wait(v, "A.t.v0")
        inst = v.tensor_copy(A["v0c"][:, 0:1], A["ps_w"][:, 0:1])
        ctx.inc(inst, "vm", 1)
        ctx.mark("A.v.v0c", "vm")
        ctx.mark("A.v.k0v0", "vm")
        def emit_qsc(qi):
            w, ic = QSC[qi]
            sal = {"g": sg_all, "u": su_all, "d": sd_all}[w]
            ral = {"g": rsg_all, "u": rsu_all, "d": rsd_all}[w]
            ctx.wait(v, f"A.nat{qi}")
            if w == "d":
                inst = v.reduce_max(A["absw"][:, 0:1], A["wnat"][:, 0:1408],
                                    axis=AX.X, apply_absolute_value=True)
                ctx.inc(inst, "vm", 1)
                ctx.mark(f"A.v.qd{qi}", "vm")
                ctx.wait(v, f"A.nat{qi}b")
                inst = v.reduce_max(A["absw"][:, 1:2], A["wnat"][:, 0:1408],
                                    axis=AX.X, apply_absolute_value=True)
                ctx.inc(inst, "vm", 1)
                ctx.selfwait(v, "vm")
                inst = v.tensor_tensor(A["absw"][:, 0:1], A["absw"][:, 0:1],
                                       A["absw"][:, 1:2], ALU.max)
                ctx.inc(inst, "vm", 1)
                ctx.selfwait(v, "vm")
            else:
                inst = v.reduce_max(A["absw"][:, 0:1], A["wnat"][:, 0:1024],
                                    axis=AX.X, apply_absolute_value=True)
                ctx.inc(inst, "vm", 1)
                ctx.selfwait(v, "vm")
            inst = v.tensor_scalar_mul(sal[:, ic:ic + 1], A["absw"][:, 0:1], 1.0 / QMAX)
            ctx.inc(inst, "vm", 1)
            ctx.selfwait(v, "vm")
            inst = v.reciprocal(ral[:, ic:ic + 1], sal[:, ic:ic + 1])
            ctx.inc(inst, "vm", 1)
            ctx.mark(f"A.v.qn{qi}", "vm")

        def emit_destride(u):
            ctx.wait(v, f"A.seg{u}")
            if u >= 2:
                ctx.wait(v, f"A.t.seg{u - 2}v")
            x3 = A["xw3"][:].rearrange("p (b w c j) -> p b w c j", b=2, w=3, c=8)
            for w in range(3):
                inst = v.tensor_copy(
                    x3[:, u % 2, w, :, 0:129],
                    A["xw"][:].rearrange("p (b c m) -> p b c m", b=2, c=8)
                    [:, u % 2, :, w:w + 385:3])
                ctx.inc(inst, "vm", 1)
            ctx.mark(f"A.v.x3{u}", "vm")

        emit_destride(0)
        for s in range(NSEG):
            if s + 1 < NSEG:
                emit_destride(s + 1)
            ctx.wait(v, f"A.t.seg{s}v")
            inst = v.tensor_copy(v_aug4()[:, s, :, 0:64],
                                 A["ps_v"][:, (s % 2) * 512:(s % 2) * 512 + 128]
                                 .rearrange("p (h c) -> p h c", h=2))
            ctx.inc(inst, "vm", 1)
            ctx.mark(f"A.v.seg{s}v", "vm")
            for qi in range(len(QSC)):
                if QSEG[qi] == s:
                    emit_qsc(qi)
        ctx.selfwait(v, "vm")
        ctx.mark("A.v.scales", "vm")
        ctx.barrier_inc(v)

    def A_scalar(s_, A):
        for s in range(NSEG):
            ctx.wait(s_, f"A.t.seg{s}k")
            inst = s_.activation(k_h[:, KS * s + 1:KS * s + 129],
                                 A["ps_k"][:, (s % 2) * 512:(s % 2) * 512 + 128],
                                 AF.Copy)
            ctx.inc(inst, "sm", 1)
            ctx.mark(f"A.sc.k{s}", "sm")
            ctx.wait(s_, f"A.t.seg{s}q")
            nq = seg_n_q(s)
            inst = s_.activation(q_h[:, 384 * s:384 * s + nq],
                                 A["ps_q"][:, (s % 2) * 512:(s % 2) * 512 + nq],
                                 AF.Copy)
            ctx.inc(inst, "sm", 1)
            ctx.mark(f"A.sc.q{s}", "sm")
        ctx.selfwait(s_, "sm")
        ctx.barrier_inc(s_)

    stB = ExitStack()

    def alloc_B():
        def sbb(name, shape, dt):
            return stB.enter_context(nc.sbuf_tensor(name, shape, dt))
        o = {}
        o["pt"] = sbb("pt", [128, 6 * 2048], BF16)
        o["oh"] = sbb("oh", [128, 1024], F32)
        o["dd"] = sbb("dd", [128, 1024], F32)
        o["bd"] = sbb("bd", [128, 1024], F32)
        o["ones1k"] = sbb("ones1k", [128, 1024], F32)
        o["ot"] = sbb("ot", [128, 4 * 1024], BF16)
        o["ps_s"] = stB.enter_context(nc.psum_tensor("ps_s", [128, 2048], F32))
        o["ps_oA"] = stB.enter_context(nc.psum_tensor("ps_oA", [128, 1024], F32))
        o["ps_oB"] = stB.enter_context(nc.psum_tensor("ps_oB", [128, 1024], F32))
        return o

    def B_tensor(t, B):
        ps_s, ps_oA, ps_oB, pt = B["ps_s"], B["ps_oA"], B["ps_oB"], B["pt"]
        ctx.wait(t, f"A.sc.k{NSEG - 1}")
        ctx.wait(t, f"A.sc.q{NSEG - 1}")
        ctx.wait(t, f"A.v.seg{NSEG - 1}v")
        ctx.wait(t, "A.v0aug")
        ctx.wait(t, "A.ginit")
        ck = [0]

        def emit_pv(ti, i):
            s = SUBS[ti][i]
            g = GID[(ti, i)]
            if mask_pieces(ti, s):
                ctx.wait(t, f"B.v.mk{g}")
            else:
                ctx.wait(t, f"B.s.ex{g}")
            if i == 0 and ti != TORDER[0]:
                ctx.wait(t, f"B.v.oc{PREV_T[ti]}")
            inst = None
            for h, pso in ((0, ps_oA), (1, ps_oB)):
                for c in range(pv_c0(ti, s), 2):
                    qs = max(512 * c, qx_of(ti, s), 0)
                    n = 512 * (c + 1) - qs
                    base = (g % 6) * 2048 + off(c, h) + (qs - 512 * c)
                    inst = t.matmul(pso[:, qs:qs + n],
                                    v_aug[:, s * 256 + h * 128:s * 256 + h * 128 + 128],
                                    pt[:, base:base + n],
                                    start=(i == 0), stop=(s == last_sub(ti, c)))
            ctx.inc(inst, "tm", 1)
            ctx.mark(f"B.t.pv{g}", "tm")

        for tp, ti in enumerate(TORDER):
            for i, s in enumerate(SUBS[ti]):
                g = GID[(ti, i)]
                ql = ql_of(ti, s)
                for c in range(ql // 512, 2):
                    kc = ck[0]; ck[0] += 1
                    if kc >= 2:
                        ctx.wait(t, f"B.s.exc{kc - 2}")
                    inst = None
                    for h in range(2):
                        inst = t.matmul(ps_s[:, (kc % 2) * 1024 + h * 512:(kc % 2) * 1024 + h * 512 + 512],
                                        k_h[64 * h:64 * h + 64, KS * s:KS * s + KS],
                                        q_h[64 * h:64 * h + 64, TQ * ti + 512 * c:TQ * ti + 512 * c + 512],
                                        start=True, stop=True, tile_position=(64 * h, 0))
                    ctx.inc(inst, "tm", 1)
                    ctx.mark(f"B.t.mmc{kc}", "tm")
                ctx.mark(f"B.t.mm{g}", "tm")
                if i > 0:
                    emit_pv(ti, i - 1)
            emit_pv(ti, len(SUBS[ti]) - 1)
        ctx.selfwait(t, "tm")
        ctx.barrier_inc(t)

    def B_scalar(s_, B):
        pt, ps_s = B["pt"], B["ps_s"]
        ctx.wait(s_, "A.v.scales")
        ck = [0]
        inst = None
        for tp, ti in enumerate(TORDER):
            for i, s in enumerate(SUBS[ti]):
                g = GID[(ti, i)]
                if g >= 7:
                    ctx.wait(s_, f"B.t.pv{g - 6}")
                ql = ql_of(ti, s)
                if ql == 0 and ck[0] % 2 == 0:
                    kc = ck[0]; ck[0] += 2
                    ctx.wait(s_, f"B.t.mmc{kc + 1}")
                    inst = s_.activation(pt[:, (g % 6) * 2048:(g % 6) * 2048 + 2048],
                                         ps_s[:], AF.Exp, scale=0.125)
                    ctx.inc(inst, "sm", 1)
                    ctx.mark(f"B.s.exc{kc}", "sm")
                    ctx.mark(f"B.s.exc{kc + 1}", "sm")
                else:
                    for c in range(ql // 512, 2):
                        kc = ck[0]; ck[0] += 1
                        ctx.wait(s_, f"B.t.mmc{kc}")
                        inst = s_.activation(pt[:, (g % 6) * 2048 + off(c, 0):(g % 6) * 2048 + off(c, 0) + 1024],
                                             ps_s[:, (kc % 2) * 1024:(kc % 2) * 1024 + 1024],
                                             AF.Exp, scale=0.125)
                        ctx.inc(inst, "sm", 1)
                        ctx.mark(f"B.s.exc{kc}", "sm")
                ctx.mark(f"B.s.ex{g}", "sm")
        ctx.selfwait(s_, "sm")
        ctx.barrier_inc(s_)

    def B_vector(v, B):
        pt, ot, oh = B["pt"], B["ot"], B["oh"]
        dd, bd, ones1k = B["dd"], B["bd"], B["ones1k"]
        ps_oA, ps_oB = B["ps_oA"], B["ps_oB"]
        ctx.barrier_wait(v, 5)
        inst = v.memset(ones1k[:], 1.0)
        ctx.inc(inst, "vm", 1)
        ctx.selfwait(v, "vm")

        def normalize(tpv):
            tiv = TORDER[tpv]
            if tpv >= 4:
                ctx.wait(v, f"B.ot{TORDER[tpv - 4]}")
            ctx.selfwait(v, "vm")
            for c0 in range(0, 1024, 256):
                inst = v.reciprocal(bd[:, c0:c0 + 256], dd[:, c0:c0 + 256])
                ctx.inc(inst, "vm", 1)
            ctx.selfwait(v, "vm")
            inst = v.tensor_tensor(ot[0:64, (tpv % 4) * 1024:(tpv % 4) * 1024 + 1024],
                                   oh[0:64, :], bd[0:64, :], ALU.mult)
            ctx.inc(inst, "vm", 1)
            inst = v.tensor_tensor(ot[64:128, (tpv % 4) * 1024:(tpv % 4) * 1024 + 1024],
                                   oh[64:128, :], bd[64:128, :], ALU.mult)
            ctx.inc(inst, "vm", 1)
            ctx.mark(f"B.v.den{tiv}", "vm")

        def quant(tp):
            for qi in QB[tp]:
                buf = qi % 4
                ctx.wait(v, f"B.qa{qi}")
                ctx.wait(v, f"B.qb{qi}")
                if qi >= 6:
                    ctx.wait(v, f"B.qst{qi - 6}")
                wv_ = wtile[:, buf * 1024:buf * 1024 + 1024]
                rv_ = rs_bc[:, buf * 1024:buf * 1024 + 1024]
                bv_ = wqb[:, (qi % 6) * 1024:(qi % 6) * 1024 + 1024]
                inst = v.tensor_mul(rv_, wv_, rv_)
                ctx.inc(inst, "vm", 1)
                ctx.selfwait(v, "vm")
                ctx.mark(f"B.g.q{qi}", "vm")
                inst = v.tensor_scalar(bv_, rv_, MAGIC, MAGIC, ALU.add, ALU.subtract)
                ctx.inc(inst, "vm", 1)
                ctx.selfwait(v, "vm")
                ctx.mark(f"B.g.qb{qi}", "vm")

        for tp, ti in enumerate(TORDER):
            for i, s in enumerate(SUBS[ti]):
                g = GID[(ti, i)]
                pieces = mask_pieces(ti, s)
                if pieces:
                    ctx.wait(v, f"B.s.ex{g}")
                    for (h, a, b) in pieces:
                        c = a // 512
                        base = (g % 6) * 2048 + off(c, h) + (a - 512 * c)
                        q0 = qx_of(ti, s)
                        inst = v.tensor_mul(pt[:, base:base + (b - a)],
                                            pt[:, base:base + (b - a)],
                                            m384[:, a - q0:b - q0])
                        ctx.inc(inst, "vm", 1)
                    ctx.mark(f"B.v.mk{g}", "vm")
                if i == min(3, len(SUBS[ti]) - 1):
                    if tp > 0:
                        normalize(tp - 1)
                    quant(tp)
            gl = GID[(ti, len(SUBS[ti]) - 1)]
            ctx.wait(v, f"B.t.pv{gl}")
            ctx.selfwait(v, "vm")
            inst = v.tensor_tensor(oh[0:64, :], ps_oA[0:64, :],
                                   ones1k[0:64, :], ALU.mult)
            ctx.inc(inst, "vm", 1)
            inst = v.tensor_tensor(oh[64:128, :], ps_oB[0:64, :],
                                   ones1k[64:128, :], ALU.mult)
            ctx.inc(inst, "vm", 1)
            inst = v.tensor_tensor(dd[0:64, :], ps_oA[64:128, :],
                                   ones1k[0:64, :], ALU.mult)
            ctx.inc(inst, "vm", 1)
            inst = v.tensor_tensor(dd[64:128, :], ps_oB[64:128, :],
                                   ones1k[64:128, :], ALU.mult)
            ctx.inc(inst, "vm", 1)
            ctx.mark(f"B.v.oc{ti}", "vm")
        normalize(NT - 1)
        ctx.selfwait(v, "vm")
        ctx.barrier_inc(v)

    def B_sync(sy, B):
        ot = B["ot"]
        ctx.barrier_wait(sy, 5)
        ctx.wait(sy, "A.rsd")
        LQ = [[] for _ in range(NT)]
        for _t in range(NT):
            for _q in QB[_t]:
                LQ[max(0, _t - 2)].append(_q)
        for tp, ti in enumerate(TORDER):
            for qi in LQ[tp]:
                w, ic = QIT[qi]
                buf = qi % 4
                if qi >= 4:
                    ctx.wait(sy, f"B.g.qb{qi - 4}")
                if w == "d":
                    src = bass.AP(wdT, ic * 128 * D, [[D, 128], [128, 8], [1, 128]])
                    rsrc = bass.AP(rs_d, 2 * DI, [[0, 128], [128, 8], [1, 128]])
                else:
                    tt = wgT if w == "g" else wuT
                    src = bass.AP(tt, ic * 128, [[DI, 128], [128 * DI, 8], [1, 128]])
                    rsrc = bass.AP(rs_d, (0 if w == "g" else DI) + ic * 128,
                                   [[0, 128], [0, 8], [1, 128]])
                ctx.sdma(sy, key=f"B.qa{qi}",
                         out=wtile[:].rearrange("p (b hc j) -> p b hc j", b=4, hc=8)[:, buf],
                         in_=src)
                ctx.sdma(sy, key=f"B.qb{qi}",
                         out=rs_bc[:].rearrange("p (b hc j) -> p b hc j", b=4, hc=8)[:, buf],
                         in_=rsrc)

            ctx.wait(sy, f"B.v.den{ti}")
            ctx.sdma(sy, key=f"B.ot{ti}",
                     out=bass.AP(o_in, (ti // 2) * 128 * (T // 8) + (ti % 2) * TQ,
                                 [[T // 8, 128], [1, TQ]]),
                     in_=ot[:, (tp % 4) * 1024:(tp % 4) * 1024 + 1024])
            if tp % 2 == 1:
                ctx.sdrain(sy)
                if not ctx.mock:
                    sy.sem_inc(ctx.sems["sy"], 1)
                ctx.cnt["sy"] = ctx.cnt.get("sy", 0) + 1
                ctx.mark(f"B.otq{ti // 2}", "sy")
        ctx.barrier_inc(sy)

    def B_gpsimd(g, B):
        ctx.barrier_wait(g, 5)
        for tp, ti in enumerate(TORDER):
            for qi in QB[tp]:
                w, ic = QIT[qi]
                ctx.wait(g, f"B.g.qb{qi}")
                inst = ctx.gdma(g, out=bass.AP(wq_d[w], ic * 8 * 128 * 128,
                                               [[128, 128], [128 * 128, 8], [1, 128]]),
                                in_=wqb[:].rearrange("p (b hc j) -> p b hc j", b=6, hc=8)[:, qi % 6])
                ctx.mark(f"B.qst{qi}", "lg")
            if tp % 2 == 1:
                q = ti // 2
                ctx.wait(g, f"B.otq{q}")
                inst = g.collective_compute("AllGather", ALU.bypass,
                                            replica_groups=[list(range(NCORE))],
                                            ins=[o_in[q]], outs=[o_ag[q]])
                ctx.inc(inst, "gm", 1)
        ctx.selfwait(g, "gm")
        ctx.barrier_inc(g)

    stC1 = ExitStack()

    def alloc_C1():
        def sc1(name, shape, dt):
            return stC1.enter_context(nc.sbuf_tensor(name, shape, dt))
        o = {}
        o["o_c"] = sc1("o_c", [128, 8 * 2048], BF16)
        o["woT_sb"] = sc1("woT_sb", [128, 8 * 1024], BF16)
        o["bo_sb"] = sc1("bo_sb", [128, 8], F32)
        o["xrs"] = sc1("xrs", [128, 8 * 512], F32)
        o["t1a"] = sc1("t1a", [128, 2 * 512], F32)
        o["t1"] = sc1("t1", [128, 8 * 512], F32)
        o["t1sq"] = sc1("t1sq", [128, 8 * 512], F32)
        o["mbc"] = sc1("mbc", [128, 512], F32)
        o["m2bc"] = sc1("m2bc", [128, 512], F32)
        o["varb"] = sc1("varb", [128, 512], F32)
        o["sdev"] = sc1("sdev", [128, 512], F32)
        o["invb"] = sc1("invb", [128, 512], F32)
        o["xln"] = sc1("xln", [128, 8 * 512], F32)
        o["xq"] = sc1("xq", [128, 8 * 2048], BF16)
        o["wf"] = sc1("wf", [128, 2 * 2048], BF16)
        o["hsil"] = sc1("hsil", [128, 2 * 512], F32)
        o["ht"] = sc1("ht", [128, 2 * 512], F32)
        o["habs"] = sc1("habs", [128, 2], F32)
        o["ps_at"] = stC1.enter_context(nc.psum_tensor("ps_at", [128, 1024], F32))
        o["ps_M"] = stC1.enter_context(nc.psum_tensor("ps_M", [128, 512], F32))
        o["ps_SQ"] = stC1.enter_context(nc.psum_tensor("ps_SQ", [128, 512], F32))
        o["ps_g"] = stC1.enter_context(nc.psum_tensor("ps_g", [128, 1024], F32))
        o["ps_u"] = stC1.enter_context(nc.psum_tensor("ps_u", [128, 1024], F32))
        return o

    def C1_gpsimd(g, C):
        ctx.barrier_wait(g, 10)
        ctx.gdma(g, out=C["woT_sb"][:].rearrange("p (hc j) -> p hc j", hc=8),
                 in_=bass.AP(woT, 0, [[D, 128], [128 * D, 8], [1, D]]))
        ctx.mark("C1.woT", "lg")
        ctx.gdma(g, out=scal[:, 12:14], in_=bass.AP(clipx, 0, [[0, 128], [0, 2], [1, 1]]))
        ctx.gdma(g, out=scal[:, 14:16], in_=bass.AP(clipd, 0, [[0, 128], [0, 2], [1, 1]]))
        ctx.mark("C1.clipraw", "lg")
        ctx.wait(g, "C1.ar1in")
        inst = g.collective_compute("AllReduce", ALU.max,
                                    replica_groups=[list(range(NCORE))],
                                    ins=[ar1_in.ap()], outs=[ar1_out.ap()])
        ctx.inc(inst, "gm", 1)
        ctx.selfwait(g, "gm")
        ctx.gdma(g, out=rrow[0:1, 0:128], in_=bass.AP(ar1_out, 0, [[1, 1], [1, 128]]))
        ctx.mark("C1.ar1row", "lg")
        ctx.wait(g, "C1.gmaxb")
        ctx.gdma(g, out=scal[:, 0:2], in_=bass.AP(sc_d, 0, [[0, 128], [0, 2], [1, 1]]))
        ctx.mark("C1.gmaxbc", "lg")
        ctx.selfwait(g, "lg")
        ctx.barrier_inc(g)

    def C1_sync(sy, C):
        ctx.barrier_wait(sy, 10)
        for c2 in range(4):
            if not ctx.mock:
                off_d = sy.value_load(coff_sb[0:1, c2:c2 + 1]) * 1
            else:
                off_d = 0
            ctx.sdma(sy, key=f"C1.oc{c2}",
                     out=C["o_c"][:].rearrange("p (r j) -> p r j", r=8)[:, 2 * c2:2 * c2 + 2],
                     in_=bass.AP(o_ag, off_d, [[T // 8, 128], [128 * (T // 8), 2], [1, TPC]]))
        ctx.sdma(sy, key="C1.bo", out=C["bo_sb"][:],
                 in_=bass.AP(bo, 0, [[1, 128], [128, 8]]),
                 allow_slow_non_contiguous=True)
        for gs in range(4):
            if gs > 0:
                ctx.wait(sy, f"C1.v.t1all{gs - 1}")
            ctx.sdma(sy, key=f"C1.xr{gs}",
                     out=C["xrs"][:].rearrange("p (dc j) -> p dc j", dc=8),
                     in_=bass.AP(xres, 512 * gs, [[TPC, 128], [128 * TPC, 8], [1, 512]]))
            ctx.wait(sy, f"C1.v.xln{gs}")
            ctx.sdma(sy, key=f"C1.xlnst{gs}",
                     out=bass.AP(xln_d, 512 * gs, [[TPC, 128], [128 * TPC, 8], [1, 512]]),
                     in_=C["xln"][:].rearrange("p (dc q) -> p dc q", dc=8))
        ctx.wait(sy, "C1.v.xmax")
        ctx.sdma(sy, key="C1.ar1in", out=bass.AP(ar1_in, 0, [[1, 128], [1, 1]]),
                 in_=rmax[:, 0:1])
        ctx.wait(sy, "C1.v.gmaxred")
        ctx.sdma(sy, key="C1.gmaxb", out=bass.AP(sc_d, 0, [[1, 1], [1, 1]]),
                 in_=rrow[0:1, 128:129])
        for gs in range(4):
            if gs == 0:
                ctx.wait(sy, "C1.xlnst3")
            else:
                ctx.wait(sy, f"C1.s.xq{gs - 1}")
            ctx.wait(sy, f"C1.xlnst{gs}")
            ctx.sdma(sy, key=f"C1.xqld{gs}",
                     out=C["xln"][:].rearrange("p (dc q) -> p dc q", dc=8),
                     in_=bass.AP(xln_d, 512 * gs, [[TPC, 128], [128 * TPC, 8], [1, 512]]))
        for ic in range(IC + 1):
            if ic < IC:
                if ic >= 2:
                    ctx.wait(sy, f"C1.t.guic{ic - 2}")
                for wi, wdsrc in ((0, wq_d["g"]), (1, wq_d["u"])):
                    ctx.sdma(sy, key=f"C1.gw{ic}{'a' if wi == 0 else 'b'}",
                             out=C["wf"][:, (ic % 2) * 2048 + wi * 1024:(ic % 2) * 2048 + wi * 1024 + 1024]
                             .rearrange("p (hc j) -> p hc j", hc=8),
                             in_=bass.AP(wdsrc, ic * 8 * 128 * 128,
                                         [[128, 128], [128 * 128, 8], [1, 128]]))
            if ic >= 1:
                for gs in range(4):
                    kk = (ic - 1) * 4 + gs
                    ctx.wait(sy, f"C1.v.h{kk}")
                    ctx.sdma(sy, key=f"C1.hst{kk}",
                             out=bass.AP(h_d, (ic - 1) * 128 * TPC + 512 * gs, [[TPC, 128], [1, 512]]),
                             in_=C["ht"][:, (kk % 2) * 512:(kk % 2) * 512 + 512])
        ctx.wait(sy, "C1.v.hmax")
        ctx.sdma(sy, key="C1.ar2in", out=bass.AP(ar2_in, 0, [[1, 128], [1, 1]]),
                 in_=rmax[:, 1:2])
        ctx.sdrain(sy)
        if not ctx.mock:
            sy.sem_inc(ctx.sems["sy"], 1)
        ctx.cnt["sy"] = ctx.cnt.get("sy", 0) + 1
        ctx.mark("C1.alldma", "sy")
        ctx.barrier_inc(sy)

    def C1_tensor(t, C):
        ctx.barrier_wait(t, 10)
        ctx.wait(t, "C1.woT")
        for _c2 in range(4):
            ctx.wait(t, f"C1.oc{_c2}")
        for gs in range(4):
            for dc in range(8):
                k = gs * 8 + dc
                if k >= 2:
                    ctx.wait(t, f"C1.s.bo{k - 2}")
                inst = None
                for hc in range(8):
                    inst = t.matmul(C["ps_at"][:, (k % 2) * 512:(k % 2) * 512 + 512],
                                    C["woT_sb"][:, hc * 1024 + dc * 128:hc * 1024 + dc * 128 + 128],
                                    C["o_c"][:, hc * 2048 + 512 * gs:hc * 2048 + 512 * gs + 512],
                                    start=(hc == 0), stop=(hc == 7))
                ctx.inc(inst, "tm", 1)
                ctx.mark(f"C1.t.at{k}", "tm")
            ctx.wait(t, f"C1.v.t1all{gs}")
            ctx.wait(t, f"C1.v.sqall{gs}")
            if gs > 0:
                ctx.wait(t, f"C1.v.var{gs - 1}")
            inst = None
            for dc in range(8):
                inst = t.matmul(C["ps_M"][:], ones128[:], C["t1"][:, dc * 512:dc * 512 + 512],
                                start=(dc == 0), stop=(dc == 7))
            for dc in range(8):
                inst = t.matmul(C["ps_SQ"][:], ones128[:], C["t1sq"][:, dc * 512:dc * 512 + 512],
                                start=(dc == 0), stop=(dc == 7))
            ctx.inc(inst, "tm", 1)
            ctx.mark(f"C1.t.st{gs}", "tm")
        for ic in range(IC):
            ctx.wait(t, f"C1.gw{ic}a")
            ctx.wait(t, f"C1.gw{ic}b")
            for gs in range(4):
                k = ic * 4 + gs
                ctx.wait(t, f"C1.v.xq{gs}")
                if k >= 4:
                    ctx.wait(t, f"C1.v.h{k - 4}")
                if k < 4:
                    ctx.wait(t, "C1.s.bo31")
                    ctx.wait(t, "C1.v.var3")
                gslot = [C["ps_g"][:, 0:512], C["ps_at"][:, 0:512],
                         C["ps_g"][:, 512:1024], C["ps_at"][:, 512:1024]][k % 4]
                uslot = [C["ps_u"][:, 0:512], C["ps_M"][:],
                         C["ps_u"][:, 512:1024], C["ps_SQ"][:]][k % 4]
                inst = None
                for hc in range(8):
                    inst = t.matmul(gslot,
                                    C["wf"][:, (ic % 2) * 2048 + hc * 128:(ic % 2) * 2048 + hc * 128 + 128],
                                    C["xq"][:, hc * 2048 + 512 * gs:hc * 2048 + 512 * gs + 512],
                                    start=(hc == 0), stop=(hc == 7))
                for hc in range(8):
                    inst = t.matmul(uslot,
                                    C["wf"][:, (ic % 2) * 2048 + 1024 + hc * 128:(ic % 2) * 2048 + 1024 + hc * 128 + 128],
                                    C["xq"][:, hc * 2048 + 512 * gs:hc * 2048 + 512 * gs + 512],
                                    start=(hc == 0), stop=(hc == 7))
                ctx.inc(inst, "tm", 1)
                ctx.mark(f"C1.t.gu{k}", "tm")
            ctx.mark(f"C1.t.guic{ic}", "tm")
        ctx.selfwait(t, "tm")
        ctx.barrier_inc(t)

    def C1_scalar(s_, C):
        ctx.barrier_wait(s_, 10)
        ctx.wait(s_, "C1.bo")
        for gs in range(4):
            for dc in range(8):
                k = gs * 8 + dc
                ctx.wait(s_, f"C1.t.at{k}")
                if k >= 2:
                    ctx.wait(s_, f"C1.v.t1_{k - 2}")
                inst = s_.activation(C["t1a"][:, (k % 2) * 512:(k % 2) * 512 + 512],
                                     C["ps_at"][:, (k % 2) * 512:(k % 2) * 512 + 512],
                                     AF.Identity, bias=C["bo_sb"][:, dc:dc + 1])
                ctx.inc(inst, "sm", 1)
                ctx.mark(f"C1.s.bo{k}", "sm")
            ctx.wait(s_, f"C1.v.var{gs}")
            inst = s_.activation(C["sdev"][:], C["varb"][:], AF.Sqrt)
            ctx.inc(inst, "sm", 1)
            ctx.mark(f"C1.s.sqrt{gs}", "sm")
        ctx.wait(s_, "C1.clipraw")
        inst = s_.activation(scal[:, 12:14], scal[:, 12:14], AF.Sigmoid)
        ctx.inc(inst, "sm", 1)
        ctx.selfwait(s_, "sm")
        inst = s_.activation(scal[:, 14:16], scal[:, 14:16], AF.Sigmoid)
        ctx.inc(inst, "sm", 1)
        ctx.mark("C1.s.sig", "sm")
        for gs in range(4):
            ctx.wait(s_, "C1.v.invs")
            ctx.wait(s_, f"C1.xqld{gs}")
            if gs > 0:
                ctx.wait(s_, f"C1.v.xq{gs - 1}")
            inst = None
            for dc in range(8):
                inst = s_.activation(C["t1sq"][:, dc * 512:dc * 512 + 512],
                                     C["xln"][:, dc * 512:dc * 512 + 512],
                                     AF.Identity, bias=magic_sb[:, 0:1], scale=scal[:, 4:5])
                ctx.inc(inst, "sm", 1)
            ctx.mark(f"C1.s.xq{gs}", "sm")
        for ic in range(IC):
            for gs in range(4):
                k = ic * 4 + gs
                ctx.wait(s_, f"C1.t.gu{k}")
                ctx.wait(s_, "C1.v.sgi")
                if k >= 2:
                    ctx.wait(s_, f"C1.v.h{k - 2}")
                gslot = [C["ps_g"][:, 0:512], C["ps_at"][:, 0:512],
                         C["ps_g"][:, 512:1024], C["ps_at"][:, 512:1024]][k % 4]
                inst = s_.activation(C["hsil"][:, (k % 2) * 512:(k % 2) * 512 + 512],
                                     gslot, AF.Silu, scale=sgi[:, ic:ic + 1])
                ctx.inc(inst, "sm", 1)
                ctx.mark(f"C1.s.si{k}", "sm")
        ctx.selfwait(s_, "sm")
        ctx.barrier_inc(s_)

    def C1_vector(v, C):
        ctx.barrier_wait(v, 10)
        for gs in range(4):
            ctx.wait(v, f"C1.xr{gs}")
            for dc in range(8):
                k = gs * 8 + dc
                ctx.wait(v, f"C1.s.bo{k}")
                inst = v.tensor_add(C["t1"][:, dc * 512:dc * 512 + 512],
                                    C["t1a"][:, (k % 2) * 512:(k % 2) * 512 + 512],
                                    C["xrs"][:, dc * 512:dc * 512 + 512])
                ctx.inc(inst, "vm", 1)
                ctx.mark(f"C1.v.t1_{k}", "vm")
                ctx.selfwait(v, "vm")
                inst = v.tensor_mul(C["t1sq"][:, dc * 512:dc * 512 + 512],
                                    C["t1"][:, dc * 512:dc * 512 + 512],
                                    C["t1"][:, dc * 512:dc * 512 + 512])
                ctx.inc(inst, "vm", 1)
            ctx.mark(f"C1.v.t1all{gs}", "vm")
            ctx.mark(f"C1.v.sqall{gs}", "vm")
            ctx.wait(v, f"C1.t.st{gs}")
            inst = v.tensor_scalar_mul(C["mbc"][:], C["ps_M"][:], 1.0 / D)
            ctx.inc(inst, "vm", 1)
            ctx.selfwait(v, "vm")
            inst = v.tensor_mul(C["m2bc"][:], C["mbc"][:], C["mbc"][:])
            ctx.inc(inst, "vm", 1)
            ctx.selfwait(v, "vm")
            inst = v.tensor_scalar_sub(C["m2bc"][:], C["m2bc"][:], 1e-5)
            ctx.inc(inst, "vm", 1)
            ctx.selfwait(v, "vm")
            inst = v.scalar_tensor_tensor(C["varb"][:], C["ps_SQ"][:], 1.0 / D, C["m2bc"][:],
                                          ALU.mult, ALU.subtract)
            ctx.inc(inst, "vm", 1)
            ctx.mark(f"C1.v.var{gs}", "vm")
            ctx.wait(v, f"C1.s.sqrt{gs}")
            for c0 in range(0, 512, 256):
                inst = v.reciprocal(C["invb"][:, c0:c0 + 256], C["sdev"][:, c0:c0 + 256])
                ctx.inc(inst, "vm", 1)
            ctx.selfwait(v, "vm")
            if gs > 0:
                ctx.wait(v, f"C1.xlnst{gs - 1}")
            for dc in range(8):
                inst = v.scalar_tensor_tensor(C["xln"][:, dc * 512:dc * 512 + 512],
                                              C["mbc"][:], -1.0, C["t1"][:, dc * 512:dc * 512 + 512],
                                              ALU.mult, ALU.add)
                ctx.inc(inst, "vm", 1)
            ctx.selfwait(v, "vm")
            for dc in range(8):
                xv = C["xln"][:, dc * 512:dc * 512 + 512]
                inst = v.tensor_mul(xv, xv, C["invb"][:])
                ctx.inc(inst, "vm", 1)
            ctx.mark(f"C1.v.xln{gs}", "vm")
            for dc in range(8):
                ctx.selfwait(v, "vm")
                inst = v.reduce_max(C["habs"][:, 0:1],
                                    C["xln"][:, dc * 512:dc * 512 + 512],
                                    axis=AX.X, apply_absolute_value=True)
                ctx.inc(inst, "vm", 1)
                ctx.selfwait(v, "vm")
                inst = v.tensor_max(rmax[:, 0:1], rmax[:, 0:1], C["habs"][:, 0:1])
                ctx.inc(inst, "vm", 1)
        ctx.mark("C1.v.xmax", "vm")
        ctx.wait(v, "C1.ar1row")
        inst = v.reduce_max(rrow[0:1, 128:129], rrow[0:1, 0:128], axis=AX.X)
        ctx.inc(inst, "vm", 1)
        ctx.selfwait(v, "vm")
        ctx.mark("C1.v.gmaxred", "vm")
        ctx.wait(v, "C1.gmaxbc")
        ctx.wait(v, "C1.s.sig")
        inst = v.tensor_mul(scal[:, 2:4], scal[:, 0:2], scal[:, 12:14])
        ctx.inc(inst, "vm", 1)
        ctx.selfwait(v, "vm")
        inst = v.tensor_scalar_mul(scal[:, 2:4], scal[:, 2:4], 1.0 / QMAX)
        ctx.inc(inst, "vm", 1)
        ctx.selfwait(v, "vm")
        inst = v.reciprocal(scal[:, 4:6], scal[:, 2:4])
        ctx.inc(inst, "vm", 1)
        ctx.selfwait(v, "vm")
        ctx.mark("C1.v.invs", "vm")
        for ic in range(IC):
            inst = v.tensor_mul(sgi[:, ic:ic + 1], scal[:, 2:3], sg_all[:, ic:ic + 1])
            ctx.inc(inst, "vm", 1)
            inst = v.tensor_mul(sui[:, ic:ic + 1], scal[:, 2:3], su_all[:, ic:ic + 1])
            ctx.inc(inst, "vm", 1)
        ctx.selfwait(v, "vm")
        ctx.mark("C1.v.sgi", "vm")
        for gs in range(4):
            ctx.wait(v, f"C1.s.xq{gs}")
            for dc in range(8):
                tv = C["t1sq"][:, dc * 512:dc * 512 + 512]
                inst = v.tensor_scalar(tv, tv, MAGIC, QMAX, ALU.subtract, ALU.min)
                ctx.inc(inst, "vm", 1)
                ctx.selfwait(v, "vm")
                inst = v.tensor_scalar_max(C["xq"][:, dc * 2048 + 512 * gs:dc * 2048 + 512 * gs + 512],
                                           tv, -QMAX - 1.0)
                ctx.inc(inst, "vm", 1)
            ctx.mark(f"C1.v.xq{gs}", "vm")
        for ic in range(IC):
            for gs in range(4):
                k = ic * 4 + gs
                ctx.wait(v, f"C1.s.si{k}")
                if k >= 2:
                    ctx.wait(v, f"C1.hst{k - 2}")
                uslot = [C["ps_u"][:, 0:512], C["ps_M"][:],
                         C["ps_u"][:, 512:1024], C["ps_SQ"][:]][k % 4]
                inst = v.scalar_tensor_tensor(C["ht"][:, (k % 2) * 512:(k % 2) * 512 + 512],
                                              C["hsil"][:, (k % 2) * 512:(k % 2) * 512 + 512],
                                              sui[:, ic:ic + 1],
                                              uslot,
                                              ALU.mult, ALU.mult)
                ctx.inc(inst, "vm", 1)
                ctx.selfwait(v, "vm")
                ctx.mark(f"C1.v.h{k}", "vm")
                inst = v.reduce_max(C["habs"][:, 1:2], C["ht"][:, (k % 2) * 512:(k % 2) * 512 + 512],
                                    axis=AX.X, apply_absolute_value=True)
                ctx.inc(inst, "vm", 1)
                ctx.selfwait(v, "vm")
                inst = v.tensor_max(rmax[:, 1:2], rmax[:, 1:2], C["habs"][:, 1:2])
                ctx.inc(inst, "vm", 1)
                ctx.selfwait(v, "vm")
        ctx.mark("C1.v.hmax", "vm")
        ctx.barrier_inc(v)

    stC2 = ExitStack()

    def alloc_C2():
        def sc2(name, shape, dt):
            return stC2.enter_context(nc.sbuf_tensor(name, shape, dt))
        o = {}
        o["hq_all"] = sc2("hq_all", [128, IC * 2048], BF16)
        o["hstage"] = sc2("hstage", [128, 2 * 2048], F32)
        o["hfma"] = sc2("hfma", [128, 2048], F32)
        o["xln2"] = sc2("xln2", [128, 8 * 512], F32)
        o["wf2"] = sc2("wf2", [128, 4 * 1024], BF16)
        o["t2a"] = sc2("t2a", [128, 2 * 512], F32)
        o["t2"] = sc2("t2", [128, 8 * 512], F32)
        o["t2sq"] = sc2("t2sq", [128, 8 * 512], F32)
        o["mbc2"] = sc2("mbc2", [128, 512], F32)
        o["m2bc2"] = sc2("m2bc2", [128, 512], F32)
        o["varb2"] = sc2("varb2", [128, 512], F32)
        o["sdev2"] = sc2("sdev2", [128, 512], F32)
        o["invb2"] = sc2("invb2", [128, 512], F32)
        o["ob"] = sc2("ob", [128, 8 * 512], F32)
        o["ps_d"] = stC2.enter_context(nc.psum_tensor("ps_d", [128, 4096], F32))
        return o

    def C2_gpsimd(g, C):
        ctx.barrier_wait(g, 15)
        ctx.wait(g, "C1.ar2in")
        inst = g.collective_compute("AllReduce", ALU.max,
                                    replica_groups=[list(range(NCORE))],
                                    ins=[ar2_in.ap()], outs=[ar2_out.ap()])
        ctx.inc(inst, "gm", 1)
        ctx.selfwait(g, "gm")
        ctx.gdma(g, out=rrow[0:1, 0:128], in_=bass.AP(ar2_out, 0, [[1, 1], [1, 128]]))
        ctx.mark("C2.ar2row", "lg")
        ctx.wait(g, "C2.hmaxb")
        ctx.gdma(g, out=scal[:, 6:8], in_=bass.AP(sc_d, 4, [[0, 128], [0, 2], [1, 1]]))
        ctx.mark("C2.hmaxbc", "lg")
        for ic in range(IC):
            if ic >= 2:
                ctx.wait(g, f"C2.s.hfma{ic - 2}")
            ctx.gdma(g, out=C["hstage"][:, (ic % 2) * 2048:(ic % 2) * 2048 + 2048],
                     in_=bass.AP(h_d, ic * 128 * TPC, [[TPC, 128], [1, 2048]]))
            ctx.mark(f"C2.hld{ic}", "lg")
        ctx.selfwait(g, "lg")
        ctx.barrier_inc(g)

    def C2_sync(sy, C):
        ctx.barrier_wait(sy, 15)
        ctx.wait(sy, "C1.alldma")
        ctx.wait(sy, "C2.v.hmaxred")
        ctx.sdma(sy, key="C2.hmaxb", out=bass.AP(sc_d, 4, [[1, 1], [1, 1]]),
                 in_=rrow[0:1, 128:129])
        def dwload(k):
            if k >= 4:
                ctx.wait(sy, f"C2.t.dn{k - 4}")
            ctx.sdma(sy, key=f"C2.dw{k}",
                     out=C["wf2"][:].rearrange("p (b hc j) -> p b hc j", b=4, hc=8)[:, k % 4],
                     in_=bass.AP(wq_d["d"], (k % IC) * 8 * 128 * 128,
                                 [[128, 128], [128 * 128, 8], [1, 128]]))

        for k in range(4):
            dwload(k)
        for gs in range(4):
            if gs > 0:
                ctx.wait(sy, f"C2.v.t2all{gs - 1}")
            ctx.sdma(sy, key=f"C2.xln{gs}",
                     out=C["xln2"][:].rearrange("p (dc j) -> p dc j", dc=8),
                     in_=bass.AP(xln_d, 512 * gs, [[TPC, 128], [128 * TPC, 8], [1, 512]]))
            for ic in range(IC):
                k = gs * IC + ic
                if k < 4:
                    continue
                dwload(k)
            ctx.wait(sy, f"C2.v.out{gs}")
            ctx.sdma(sy, key=f"C2.out{gs}",
                     out=bass.AP(outT, 512 * gs, [[TPC, 128], [128 * TPC, 8], [1, 512]]),
                     in_=C["ob"][:].rearrange("p (dc j) -> p dc j", dc=8))
        ctx.sdrain(sy)

    def C2_tensor(t, C):
        ctx.barrier_wait(t, 15)
        for gs in range(4):
            for ic in range(IC):
                k = gs * IC + ic
                ctx.wait(t, f"C2.dw{k}")
                ctx.wait(t, f"C2.v.hq{ic}")
                if ic == 0 and gs > 0:
                    ctx.wait(t, f"C2.s.dscall{gs - 1}")
                    ctx.wait(t, f"C2.v.var2_{gs - 1}")
                inst = None
                for hc in range(8):
                    inst = t.matmul(C["ps_d"][:, hc * 512:hc * 512 + 512],
                                    C["wf2"][:, (k % 4) * 1024 + hc * 128:(k % 4) * 1024 + hc * 128 + 128],
                                    C["hq_all"][:, ic * 2048 + 512 * gs:ic * 2048 + 512 * gs + 512],
                                    start=(ic == 0), stop=(ic == IC - 1))
                ctx.inc(inst, "tm", 1)
                ctx.mark(f"C2.t.dn{k}", "tm")
            ctx.wait(t, f"C2.v.t2all{gs}")
            ctx.wait(t, f"C2.v.sq2all{gs}")
            inst = None
            for dc in range(8):
                inst = t.matmul(C["ps_d"][:, 0:512], ones128[:], C["t2"][:, dc * 512:dc * 512 + 512],
                                start=(dc == 0), stop=(dc == 7))
            for dc in range(8):
                inst = t.matmul(C["ps_d"][:, 512:1024], ones128[:], C["t2sq"][:, dc * 512:dc * 512 + 512],
                                start=(dc == 0), stop=(dc == 7))
            ctx.inc(inst, "tm", 1)
            ctx.mark(f"C2.t.st{gs}", "tm")
        ctx.selfwait(t, "tm")
        ctx.barrier_inc(t)

    def C2_scalar(s_, C):
        ctx.barrier_wait(s_, 15)
        for ic in range(IC):
            ctx.wait(s_, f"C2.hld{ic}")
            ctx.wait(s_, "C2.v.invs2")
            if ic >= 1:
                ctx.wait(s_, f"C2.v.hq{ic - 1}")
            inst = s_.activation(C["hfma"][:], C["hstage"][:, (ic % 2) * 2048:(ic % 2) * 2048 + 2048],
                                 AF.Identity, bias=magic_sb[:, 0:1], scale=scal[:, 10:11])
            ctx.inc(inst, "sm", 1)
            ctx.mark(f"C2.s.hfma{ic}", "sm")
        for gs in range(4):
            for dc in range(8):
                k = gs * 8 + dc
                ctx.wait(s_, f"C2.t.dn{gs * IC + IC - 1}")
                ctx.wait(s_, "C2.v.sdi")
                if k >= 2:
                    ctx.wait(s_, f"C2.v.t2_{k - 2}")
                inst = s_.activation(C["t2a"][:, (k % 2) * 512:(k % 2) * 512 + 512],
                                     C["ps_d"][:, dc * 512:dc * 512 + 512],
                                     AF.Identity, scale=sdi[:, dc:dc + 1])
                ctx.inc(inst, "sm", 1)
                ctx.mark(f"C2.s.dsc{k}", "sm")
            ctx.mark(f"C2.s.dscall{gs}", "sm")
            ctx.wait(s_, f"C2.v.var2_{gs}")
            inst = s_.activation(C["sdev2"][:], C["varb2"][:], AF.Sqrt)
            ctx.inc(inst, "sm", 1)
            ctx.mark(f"C2.s.sqrt2{gs}", "sm")
        ctx.selfwait(s_, "sm")
        ctx.barrier_inc(s_)

    def C2_vector(v, C):
        ctx.barrier_wait(v, 15)
        ctx.wait(v, "C2.ar2row")
        inst = v.reduce_max(rrow[0:1, 128:129], rrow[0:1, 0:128], axis=AX.X)
        ctx.inc(inst, "vm", 1)
        ctx.selfwait(v, "vm")
        ctx.mark("C2.v.hmaxred", "vm")
        ctx.wait(v, "C2.hmaxbc")
        inst = v.tensor_mul(scal[:, 8:10], scal[:, 6:8], scal[:, 14:16])
        ctx.inc(inst, "vm", 1)
        ctx.selfwait(v, "vm")
        inst = v.tensor_scalar_mul(scal[:, 8:10], scal[:, 8:10], 1.0 / QMAX)
        ctx.inc(inst, "vm", 1)
        ctx.selfwait(v, "vm")
        ctx.mark("C2.v.s2", "vm")
        inst = v.reciprocal(scal[:, 10:12], scal[:, 8:10])
        ctx.inc(inst, "vm", 1)
        ctx.selfwait(v, "vm")
        ctx.mark("C2.v.invs2", "vm")
        for dc in range(8):
            inst = v.tensor_mul(sdi[:, dc:dc + 1], scal[:, 8:9], sd_all[:, dc:dc + 1])
            ctx.inc(inst, "vm", 1)
        ctx.selfwait(v, "vm")
        ctx.mark("C2.v.sdi", "vm")
        for ic in range(IC):
            ctx.wait(v, f"C2.s.hfma{ic}")
            inst = v.tensor_scalar(C["hfma"][:], C["hfma"][:], MAGIC, QMAX, ALU.subtract, ALU.min)
            ctx.inc(inst, "vm", 1)
            ctx.selfwait(v, "vm")
            inst = v.tensor_scalar_max(C["hq_all"][:, ic * 2048:ic * 2048 + 2048], C["hfma"][:],
                                       -QMAX - 1.0)
            ctx.inc(inst, "vm", 1)
            ctx.mark(f"C2.v.hq{ic}", "vm")
        for gs in range(4):
            ctx.wait(v, f"C2.xln{gs}")
            for dc in range(8):
                k = gs * 8 + dc
                ctx.wait(v, f"C2.s.dsc{k}")
                inst = v.tensor_add(C["t2"][:, dc * 512:dc * 512 + 512],
                                    C["t2a"][:, (k % 2) * 512:(k % 2) * 512 + 512],
                                    C["xln2"][:, dc * 512:dc * 512 + 512])
                ctx.inc(inst, "vm", 1)
                ctx.mark(f"C2.v.t2_{k}", "vm")
                ctx.selfwait(v, "vm")
                inst = v.tensor_mul(C["t2sq"][:, dc * 512:dc * 512 + 512],
                                    C["t2"][:, dc * 512:dc * 512 + 512],
                                    C["t2"][:, dc * 512:dc * 512 + 512])
                ctx.inc(inst, "vm", 1)
            ctx.mark(f"C2.v.t2all{gs}", "vm")
            ctx.mark(f"C2.v.sq2all{gs}", "vm")
            ctx.wait(v, f"C2.t.st{gs}")
            inst = v.tensor_scalar_mul(C["mbc2"][:], C["ps_d"][:, 0:512], 1.0 / D)
            ctx.inc(inst, "vm", 1)
            ctx.selfwait(v, "vm")
            inst = v.tensor_mul(C["m2bc2"][:], C["mbc2"][:], C["mbc2"][:])
            ctx.inc(inst, "vm", 1)
            ctx.selfwait(v, "vm")
            inst = v.tensor_scalar_sub(C["m2bc2"][:], C["m2bc2"][:], 1e-5)
            ctx.inc(inst, "vm", 1)
            ctx.selfwait(v, "vm")
            inst = v.scalar_tensor_tensor(C["varb2"][:], C["ps_d"][:, 512:1024], 1.0 / D, C["m2bc2"][:],
                                          ALU.mult, ALU.subtract)
            ctx.inc(inst, "vm", 1)
            ctx.mark(f"C2.v.var2_{gs}", "vm")
            ctx.wait(v, f"C2.s.sqrt2{gs}")
            for c0 in range(0, 512, 256):
                inst = v.reciprocal(C["invb2"][:, c0:c0 + 256], C["sdev2"][:, c0:c0 + 256])
                ctx.inc(inst, "vm", 1)
            ctx.selfwait(v, "vm")
            if gs > 0:
                ctx.wait(v, f"C2.out{gs - 1}")
            for dc in range(8):
                inst = v.scalar_tensor_tensor(C["ob"][:, dc * 512:dc * 512 + 512],
                                              C["mbc2"][:], -1.0, C["t2"][:, dc * 512:dc * 512 + 512],
                                              ALU.mult, ALU.add)
                ctx.inc(inst, "vm", 1)
            ctx.selfwait(v, "vm")
            for dc in range(8):
                obv = C["ob"][:, dc * 512:dc * 512 + 512]
                inst = v.tensor_mul(obv, obv, C["invb2"][:])
                ctx.inc(inst, "vm", 1)
            ctx.mark(f"C2.v.out{gs}", "vm")
        ctx.barrier_inc(v)

    def DBG_A_sync(sy, _):
        ctx.barrier_wait(sy, 5)
        for key, srcap in (("kh", k_h[:, 0:NKS * KS]), ("va", v_aug[:]), ("qh", q_h[:])):
            ctx.sdma(sy, out=dbg[key].ap(), in_=srcap)
        ctx.barrier_wait(sy, 10)
        ctx.sdma(sy, out=dbg["wq"].ap(), in_=wq_d["g"].ap())
        ctx.sdma(sy, out=dbg["wd"].ap(), in_=wq_d["d"].ap())
        ctx.sdrain(sy)

    def DBG_B_sync(sy, _):
        ctx.barrier_wait(sy, 10)
        ctx.sdma(sy, out=dbg["oin"].ap(), in_=o_in.ap())
        ctx.sdrain(sy)

    def run_phase(alloc_fn, emitters, close_stacks=()):
        objs = alloc_fn() if alloc_fn else None
        import os
        if os.environ.get("KDBG"):
            print("PHASE", emitters[0][1].__name__, "sbuf remaining:", nc.sbuf_bytes_remaining)
        snap = dict(ctx.cnt)
        ctx.mock = True
        for _, fn in emitters:
            fn(_MOCK, objs)
        ctx.cnt = snap
        ctx.mock = False
        with nc.Block() as blk:
            for name, fn in emitters:
                getattr(blk, name)(lambda eng, _fn=fn: _fn(eng, objs))
        for st_ in close_stacks:
            st_.close()

    phA = [("gpsimd", A_gpsimd), ("sync", A_sync), ("tensor", A_tensor),
           ("vector", A_vector), ("scalar", A_scalar)]
    phB = [("tensor", B_tensor), ("scalar", B_scalar), ("vector", B_vector),
           ("sync", B_sync), ("gpsimd", B_gpsimd)]
    phC1 = [("gpsimd", C1_gpsimd), ("sync", C1_sync), ("tensor", C1_tensor),
            ("scalar", C1_scalar), ("vector", C1_vector)]
    phC2 = [("gpsimd", C2_gpsimd), ("sync", C2_sync), ("tensor", C2_tensor),
            ("scalar", C2_scalar), ("vector", C2_vector)]

    if stage == "A":
        run_phase(alloc_A, phA, (stA,))
        run_phase(alloc_B, phB, (stB,))
        run_phase(None, [("sync", DBG_A_sync)])
    elif stage == "B":
        run_phase(alloc_A, phA, (stA,))
        run_phase(alloc_B, phB, (stB,))
        run_phase(None, [("sync", DBG_B_sync)])
    else:
        run_phase(alloc_A, phA, (stA,))
        run_phase(alloc_B, phB, (stB, stAB))
        run_phase(alloc_C1, phC1, (stC1,))
        run_phase(alloc_C2, phC2, ())
    return nc


_NC_CACHE = {}

def _get_nc(stage=None):
    if stage not in _NC_CACHE:
        _NC_CACHE[stage] = build_nc(stage)
    return _NC_CACHE[stage]


def make_in_maps(inputs):
    x = np.ascontiguousarray(np.asarray(inputs["L0_attn_x"], dtype=np.float32)[0].T)
    wq = np.asarray(inputs["L0_attn_Wq"], dtype=np.float32)
    wk = np.asarray(inputs["L0_attn_Wk"], dtype=np.float32)
    wv = np.asarray(inputs["L0_attn_Wv"], dtype=np.float32)
    wo = np.asarray(inputs["L0_attn_Wo"], dtype=np.float32)
    bo_ = np.asarray(inputs["L0_attn_bo"], dtype=np.float32)
    wcv = np.asarray(inputs["L0_attn_Wconv"], dtype=np.float32)
    wconvN_ = np.ascontiguousarray(wcv.transpose(0, 2, 1).reshape(D, 3072))
    wqT = np.ascontiguousarray(wq.T)
    wkT = np.ascontiguousarray(wk.T)
    wvT = np.ascontiguousarray(wv.T)
    woT_ = np.ascontiguousarray(wo.T)
    wg = np.asarray(inputs["L0_ffn_w_gate"], dtype=np.float32)
    wu = np.asarray(inputs["L0_ffn_w_up"], dtype=np.float32)
    wd = np.asarray(inputs["L0_ffn_w_down"], dtype=np.float32)
    maps = []
    for c in range(NCORE):
        maps.append({
            "xT": x,
            "xres": np.ascontiguousarray(x[:, TPC * c:TPC * (c + 1)]),
            "cid": np.array([[c]], dtype=np.int32),
            "coff": np.array([[c * (128 * NCORE) * (T // 8) + c2 * 2 * 128 * (T // 8)
                               for c2 in range(4)]], dtype=np.int32),
            "wkTc": np.ascontiguousarray(wkT[:, 128 * c:128 * c + 128]),
            "wvTc": np.ascontiguousarray(wvT[:, 128 * c:128 * c + 128]),
            "wqTc": np.ascontiguousarray(wqT[:, 128 * c:128 * c + 128]),
            "wconvN": wconvN_,
            "woT": woT_,
            "bo": bo_,
            "wgN": wg, "wgT": np.ascontiguousarray(wg.T),
            "wuN": wu, "wuT": np.ascontiguousarray(wu.T),
            "wdN": wd, "wdT": np.ascontiguousarray(wd.T),
            "clipx": np.asarray(inputs["L0_ffn_clip_x"], dtype=np.float32),
            "clipd": np.asarray(inputs["L0_ffn_clip_down"], dtype=np.float32),
        })
    return maps


def kernel(**inputs):
    nc = _get_nc(None)
    maps = make_in_maps(inputs)
    res = run_bass_kernel_spmd(nc, maps, core_ids=list(range(NCORE)))
    out = np.empty((1, T, D), dtype=np.float32)
    for c in range(NCORE):
        out[0, TPC * c:TPC * (c + 1), :] = res.results[c]["outT"].T
    return out
